# version 28
# baseline (speedup 1.0000x reference)
import contextlib
import numpy as np
import concourse.bass as bass
import concourse.mybir as mybir

F32 = mybir.dt.float32
BF16 = mybir.dt.bfloat16
AF = mybir.ActivationFunctionType
ALU = mybir.AluOpType

D = 1024
S = 4096
H = 16
DH = 64
FF = 4096
NC = D // 128
NPAIR = H // 2
TFF = 256
NTFF = S // TFF
TB = 512
NTB = S // TB
NKC = S // 128
LN_EPS = 1e-5
LOG_EIGHTH = float(np.log(0.125))


def rep_view(t, col0, T, n):
    W = t.shape[1]
    return bass.AP(t, col0, [[W, 128], [0, n], [1, T]])

ENGINES = ("sync", "tensor", "vector", "scalar", "gpsimd")


class Plan:

    def __init__(self, nc, ctx):
        self.nc = nc
        self.ctx = ctx
        self.entries = {e: [] for e in ENGINES}
        self.sems = {}
        self.counts = {}
        for s in ("s_te", "s_act", "s_dve", "s_gp"):
            self._mksem(s)

    def _mksem(self, s):
        if s not in self.sems:
            self.sems[s] = self.ctx.enter_context(self.nc.semaphore(s))
            self.counts[s] = 0

    _CHAIN = {"vector": "s_dve", "scalar": "s_act", "gpsimd": "s_gp"}

    def op(self, engine, fn, waits=(), inc=None, amt=1):
        flat = []
        for w in waits:
            if w is None:
                continue
            if isinstance(w, list):
                flat.extend(x for x in w if x is not None)
            else:
                flat.append(w)
        waits = flat
        if engine in self._CHAIN and fn is not None:
            s = self._CHAIN[engine]
            if inc is None:
                inc = s
            if self.counts.get(s, 0) > 0:
                waits.append((s, self.counts[s]))
        tok = None
        if inc is not None:
            self._mksem(inc)
            self.counts[inc] += amt
            tok = (inc, self.counts[inc])
        self.entries[engine].append((fn, waits, inc, amt))
        return tok

    def dma(self, engine, out, in_, waits=(), inc=None):
        assert inc is not None
        def fn(eng, out=out, in_=in_):
            return eng.dma_start(out=out, in_=in_)
        return self.op(engine, fn, waits=waits, inc=inc, amt=16)

    def wait(self, engine, waits):
        self.entries[engine].append((None, [w for w in waits if w is not None],
                                     None, 0))

    def barrier(self):
        toks = [(s, self.counts[s]) for s in self.sems if self.counts[s] > 0]
        for e in ENGINES:
            self.wait(e, toks)

    def emit(self):
        nc = self.nc
        with nc.Block() as block:
            def mk(name):
                chain = self._CHAIN.get(name)
                def run(eng):
                    hwm = {}
                    mirror = 0
                    for fn, waits, inc, amt in self.entries[name]:
                        for (sname, val) in waits:
                            if hwm.get(sname, -1) >= val:
                                continue
                            hwm[sname] = val
                            eng.wait_ge(self.sems[sname], val)
                        if fn is None:
                            continue
                        inst = fn(eng)
                        if inc is not None:
                            inst.then_inc(self.sems[inc], amt)
                            if inc == chain:
                                mirror += amt
                return run
            block.sync(mk("sync"))
            block.tensor(mk("tensor"))
            block.vector(mk("vector"))
            block.scalar(mk("scalar"))
            block.gpsimd(mk("gpsimd"))


class LNEmitter:

    def __init__(self, p, nc, ctx, tag, T, shared, xb=None, ceng="vector",
                 bcmode="dma", ones_row_f=None, sq_eng=None):
        self.p = p
        self.T = T
        self.tag = tag
        self.ceng = ceng
        self.sq_eng = sq_eng or ceng
        self.bcmode = bcmode
        self.ones_row_f = ones_row_f
        (self.ones_col_b, self.c_eps, self.c_zero) = shared
        sb = lambda *a: ctx.enter_context(nc.sbuf_tensor(*a))
        pt = lambda *a: ctx.enter_context(nc.psum_tensor(*a))
        self.xb = xb if xb is not None else sb(f"{tag}_lnxb", [128, NC * T], BF16)
        self.r_mu = sb(f"{tag}_rmu", [1, T], F32)
        self.r_var = sb(f"{tag}_rvar", [1, T], F32)
        self.r_ab = sb(f"{tag}_rab", [1, 2 * T], F32)
        self.ps_st = pt(f"{tag}_pst", [1, 2 * T], F32)
        if bcmode == "psum":
            self.bc = pt(f"{tag}_bc", [128, 2 * T], F32)
        else:
            self.bc = sb(f"{tag}_bc", [128, 2 * T], F32)
        self.dram = nc.dram_tensor(f"{tag}_rows", [2, 2 * T], F32, kind="Internal")
        self.xb_free = None
        self.ps_free = None
        self.rows_free = None
        self.dram_free = [None, None]
        self.bc_user = None
        self.inv = 0
        self.pending = []

    def stats(self, p, xsrc, deps):
        T = self.T
        w = list(deps) + [self.xb_free]
        def cast_fn(eng):
            return eng.tensor_copy(out=self.xb[:, :NC * T], in_=xsrc[:, :NC * T])
        t_cast = p.op(self.ceng, cast_fn, waits=w)
        t_mm1 = None
        for c in range(NC):
            w = [t_cast]
            if c == 0 and self.ps_free is not None:
                w.append(self.ps_free)
            def m1(eng, c=c):
                return eng.matmul(self.ps_st[:, :T], self.ones_col_b[:, :],
                                  self.xb[:, c * T:c * T + T],
                                  start=(c == 0), stop=(c == NC - 1))
            t_mm1 = p.op("tensor", m1, waits=w,
                         inc="s_te" if c == NC - 1 else None)
        def sq_fn(eng):
            return eng.tensor_mul(out=self.xb[:, :NC * T],
                                  in0=self.xb[:, :NC * T],
                                  in1=self.xb[:, :NC * T])
        t_sq = p.op(self.sq_eng, sq_fn, waits=[t_mm1])
        t_sums = None
        for c in range(NC):
            def m2(eng, c=c):
                return eng.matmul(self.ps_st[:, T:], self.ones_col_b[:, :],
                                  self.xb[:, c * T:c * T + T],
                                  start=(c == 0), stop=(c == NC - 1))
            t_sums = p.op("tensor", m2, waits=[t_sq],
                          inc="s_te" if c == NC - 1 else None)
        self.xb_free = t_sums
        w = [t_sums, self.rows_free]
        def row1a(eng):
            eng.tensor_scalar_mul(self.r_mu[0:1, :], self.ps_st[0:1, :T], 1.0 / D)
            return eng.tensor_scalar_mul(self.r_var[0:1, :], self.ps_st[0:1, T:],
                                         1.0 / D)
        t_r1a = p.op("vector", row1a, waits=w, inc="s_dve")
        self.ps_free = t_r1a
        def row1b(eng):
            return eng.scalar_tensor_tensor(
                out=self.r_ab[0:1, self.T:], in0=self.r_mu[0:1, :], scalar=-1.0,
                in1=self.r_mu[0:1, :], op0=ALU.mult, op1=ALU.mult)
        p.op("vector", row1b, inc="s_dve")
        def row1c(eng):
            return eng.tensor_add(self.r_var[0:1, :], self.r_var[0:1, :],
                                  self.r_ab[0:1, self.T:])
        t_row1 = p.op("vector", row1c, inc="s_dve")
        def row2a(eng):
            return eng.activation(out=self.r_var[0:1, :], in_=self.r_var[0:1, :],
                                  func=AF.Ln, bias=self.c_eps[0:1, 0:1], scale=1.0)
        t_row2a = p.op("scalar", row2a, waits=[t_row1], inc="s_act")
        def row2b(eng):
            return eng.activation(out=self.r_ab[0:1, :self.T], in_=self.r_var[0:1, :],
                                  func=AF.Exp, bias=self.c_zero[0:1, 0:1], scale=-0.5)
        t_row2 = p.op("scalar", row2b, inc="s_act")
        def row3(eng):
            return eng.scalar_tensor_tensor(
                out=self.r_ab[0:1, self.T:], in0=self.r_mu[0:1, :], scalar=-1.0,
                in1=self.r_ab[0:1, :self.T], op0=ALU.mult, op1=ALU.mult)
        t_row3 = p.op("vector", row3, waits=[t_row2], inc="s_dve")
        par = self.inv % 2
        self.inv += 1
        if self.bcmode == "psum":
            self.pending.append((t_row3, par))
            return t_row3
        t_ro = p.dma("scalar", self.dram[par:par + 1, :], self.r_ab[0:1, :],
                     waits=[t_row3, self.dram_free[par]], inc=f"{self.tag}ro{par}")
        self.rows_free = t_ro
        self.pending.append((t_ro, par))
        return t_ro

    def bcast_in(self, p):
        T = self.T
        t_ro, par = self.pending.pop(0)
        if self.bcmode == "psum":
            def bcmm(eng):
                eng.matmul(self.bc[:, :T], self.ones_row_f[0:1, :],
                           self.r_ab[0:1, :T], start=True, stop=True)
                return eng.matmul(self.bc[:, T:], self.ones_row_f[0:1, :],
                                  self.r_ab[0:1, T:], start=True, stop=True)
            t_bi = p.op("tensor", bcmm, waits=[t_ro, self.bc_user], inc="s_te")
            self.dram_free[par] = t_bi
            self.rows_free = t_bi
            return t_bi
        src_bc = bass.AP(self.dram, par * 2 * T, [[0, 128], [1, 2 * T]])
        t_bi = p.dma("scalar", self.bc[:, :], src_bc,
                     waits=[t_ro, self.bc_user], inc=f"{self.tag}bi")
        self.dram_free[par] = t_bi
        return t_bi

    def set_bc_user(self, tok):
        self.bc_user = tok


def build(phases="ABC", dev=False):
    nc = bass.Bass("TRN2", target_bir_lowering=False)
    ctx = contextlib.ExitStack()
    p = Plan(nc, ctx)

    ei = lambda n, s, d: nc.dram_tensor(n, s, d, kind="ExternalInput")
    xT = ei("xT", [D, S], F32)
    W1 = ei("W1", [D, FF], F32)
    W2 = ei("W2", [FF, D], F32)
    b1d = ei("b1d", [128, FF // 128], F32)
    b2d = ei("b2d", [128, NC], F32)
    gammad = ei("gammad", [128, NC], F32)
    betad = ei("betad", [128, NC], F32)
    WqT = ei("WqT", [D, D], F32)
    WkT = ei("WkT", [D, D], F32)
    WvT = ei("WvT", [D, D], F32)
    bqd = ei("bqd", [128, NPAIR], F32)
    bkd = ei("bkd", [128, NPAIR], F32)
    bvF = ei("bvF", [1, D], F32)
    WoT = ei("WoT", [D, D], F32)
    bod = ei("bod", [128, NC], F32)

    last = phases[-1]
    def sc(name, shape, dt, out_when):
        kind = "ExternalOutput" if (dev and last == out_when) else "Internal"
        return nc.dram_tensor(name, shape, dt, kind=kind)
    W1bh = nc.dram_tensor("W1bh", [128, NC * FF], BF16, kind="Internal")
    W2bh = nc.dram_tensor("W2bh", [128, (FF // 128) * D], BF16, kind="Internal")
    x1T = sc("x1T", [D, S], F32, "A")
    ln1T = sc("ln1T", [D, S], BF16, "A")
    QsT = nc.dram_tensor("QsT", [D, S], BF16, kind="Internal")
    x2T = sc("x2T", [D, S], F32, "B")
    yT = nc.dram_tensor("yT", [D, S], F32,
                        kind="ExternalOutput" if "C" in phases else "Internal")

    sb = lambda *a: ctx.enter_context(nc.sbuf_tensor(*a))
    ones_col_b = sb("ones_col_b", [128, 1], BF16)
    ones_row_f = sb("ones_row_f", [1, 128], F32)
    c_eps = sb("c_eps", [1, 1], F32)
    c_zero = sb("c_zero", [1, 1], F32)
    c_log8 = sb("c_log8", [1, 1], F32)
    b1s = sb("b1s", [128, FF // 128], F32)
    b2s = sb("b2s", [128, NC], F32)
    gh = sb("gh", [128, NC], F32)
    bh = sb("bh", [128, NC], F32)
    bqs = sb("bqs", [128, NPAIR], F32)
    bks = sb("bks", [128, NPAIR], F32)
    bos = sb("bos", [128, NC], F32)
    qmax2 = sb("qmax2", [1, H], F32)
    kmax2 = sb("kmax2", [1, H], F32)
    mred = sb("mred", [1, 2], F32)
    negc = sb("negc", [1, H], F32)
    cln = sb("cln", [1, 2 * H], F32)
    cbias = sb("cbias", [128, H], F32)

    tprm = None
    for dst, src in ((b1s, b1d), (b2s, b2d), (gh, gammad), (bh, betad),
                     (bqs, bqd), (bks, bkd), (bos, bod)):
        tprm = p.dma("sync", dst[:, :], src[:, :], inc="s_prm")

    def init_fn(eng):
        eng.memset(ones_col_b[:, :], 1.0)
        eng.memset(ones_row_f[:, :], 1.0)
        eng.memset(c_eps[:, :], LN_EPS)
        eng.memset(c_zero[:, :], 0.0)
        eng.memset(c_log8[:, :], LOG_EIGHTH)
        eng.memset(qmax2[:, :], 0.0)
        eng.memset(kmax2[:, :], 0.0)
        eng.tensor_scalar_mul(gh[:, :], gh[:, :], 0.5)
        return eng.tensor_scalar_mul(bh[:, :], bh[:, :], 0.5)
    t_init = p.op("vector", init_fn, waits=[tprm], inc="s_dve")

    def ffn_phase(tag, in_x, in_tgt, out_x, out_ln, t_start, save_bf16=False,
                  bf16_src=False):
        es = contextlib.ExitStack()
        sbl = lambda *a: es.enter_context(nc.sbuf_tensor(*a))
        ptl = lambda *a: es.enter_context(nc.psum_tensor(*a))
        W1b = sbl(f"{tag}W1b", [128, NC * FF], BF16)
        W2b = sbl(f"{tag}W2b", [128, (FF // 128) * D], BF16)
        xstage = [sbl(f"{tag}xs{i}", [128, NC * TFF], F32) for i in range(3)]
        tgt = [sbl(f"{tag}tg{i}", [128, NC * TFF], BF16) for i in range(2)]
        x2st = sbl(f"{tag}x2", [128, NC * TFF], F32)
        wst = nc.alloc_sbuf_tensor_at(
            f"{tag}wst", [128, 2 * 1024], F32,
            offset=nc.lookup_mloc(x2st).addr)
        hsb = sbl(f"{tag}h", [128, 32 * TFF], BF16)
        f2scr = nc.alloc_sbuf_tensor_at(
            f"{tag}f2s", [128, NC * TFF], F32,
            offset=nc.lookup_mloc(hsb).addr + 16 * TFF * 2)
        xbB = sbl(f"{tag}xbB", [128, NC * TFF], BF16)
        sc = sbl(f"{tag}sc", [128, NC * TFF // 2], F32) if out_ln is not None else None
        prev_uw = [None]
        ps_h = [ptl(f"{tag}ph{i}", [128, TFF], F32) for i in range(2)]
        ps_eo = [ptl(f"{tag}pe{i}", [128, TFF], F32) for i in range(3)]
        lna = LNEmitter(p, nc, es, tag + "a", TFF, (ones_col_b, c_eps, c_zero),
                        ceng="gpsimd", sq_eng="vector")
        lnb = LNEmitter(p, nc, es, tag + "b", TFF, (ones_col_b, c_eps, c_zero),
                        xb=xbB, bcmode="psum", ones_row_f=ones_row_f)

        pre_x = {}
        for t in range(min(3, NTFF)):
            pre_x[t] = p.dma(
                "sync", xstage[t % 3].rearrange("p (c j) -> p c j", j=TFF),
                in_x.rearrange("(c p) t -> p c t", p=128)[:, :, t * TFF:(t + 1) * TFF],
                waits=[t_start], inc=f"{tag}_x{t % 3}")

        w1_tok = {}
        w2_tok = {}
        if bf16_src:
            for fb in range(4):
                toks = []
                for li, q in enumerate(("sync", "gpsimd")):
                    toks.append(p.dma(
                        q,
                        W1b.ap().rearrange("p (e f) -> p e f", f=FF)
                        [:, :, fb * 1024 + li * 512:fb * 1024 + (li + 1) * 512],
                        W1bh.ap().rearrange("p (e f) -> p e f", f=FF)
                        [:, :, fb * 1024 + li * 512:fb * 1024 + (li + 1) * 512],
                        waits=[t_start], inc=f"{tag}_wh{q[0]}{fb}"))
                for ec in range(NC):
                    w1_tok[(ec, fb)] = toks
            w2_all = [[p.dma("sync", W2b[:, :16 * D], W2bh[:, :16 * D],
                             waits=[t_start], inc=f"{tag}_w2hs"),
                       p.dma("gpsimd", W2b[:, 16 * D:], W2bh[:, 16 * D:],
                             waits=[t_start], inc=f"{tag}_w2hg")]]
        pend = []
        deferred = []
        ndma = [0]
        def flush_one():
            cfn, tdma, key, tokmap = deferred.pop(0)
            tcast = p.op("gpsimd", cfn, waits=[tdma], inc="s_gp")
            pend.append(tcast)
            if tokmap is not None:
                tokmap[key] = tcast
            return tcast
        def stream(dst, dsto, src, key, tokmap):
            i = ndma[0]
            ndma[0] += 1
            slot = i % 2
            w = [t_start] if i == 0 else []
            if i >= 2:
                w.append(pend[i - 2])
            tdma = p.dma("sync", wst[:, slot * 1024:slot * 1024 + 1024], src,
                         waits=w, inc=f"{tag}_w{slot}")
            def cfn(eng, dst=dst, dsto=dsto, slot=slot):
                return eng.tensor_copy(out=dst[:, dsto:dsto + 1024],
                                       in_=wst[:, slot * 1024:slot * 1024 + 1024])
            deferred.append((cfn, tdma, key, tokmap))
            if len(deferred) > 1:
                flush_one()
        if not bf16_src:
            for fb in range(FF // 1024):
                for ec in range(NC):
                    stream(W1b, ec * FF + fb * 1024,
                           W1[ec * 128:(ec + 1) * 128, fb * 1024:(fb + 1) * 1024],
                           (ec, fb), w1_tok)
            for fc in range(FF // 128):
                stream(W2b, fc * D,
                       W2[fc * 128:(fc + 1) * 128, :],
                       (fc, 0), w2_tok)
            w2_all = [flush_one()]
        if save_bf16:
            p.dma("gpsimd", W1bh[:, :], W1b[:, :], waits=[w2_all[0]],
                  inc=f"{tag}_wsv0")
            t_sv = p.dma("gpsimd", W2bh[:, :], W2b[:, :], waits=[w2_all[0]],
                         inc=f"{tag}_wsv1")

        tok_tile_done = [[t_start]] * NTFF
        tok_tgt_done = [None] * NTFF
        tok_eo_cons = {}
        tok_h_cons = [None] * NTFF
        t_lnout_free = None
        t_tgt_tok = [None] * NTFF
        prev_fin = [None]
        frontends = {}

        f1_bc = {}

        def frontend1(t):
            sl3 = t % 3
            t0 = t * TFF
            if t in pre_x:
                t_xin = pre_x[t]
            else:
                w = [t_start]
                if t >= 3:
                    w.extend(tok_tile_done[t - 3])
                t_xin = p.dma(
                    "sync", xstage[sl3].rearrange("p (c j) -> p c j", j=TFF),
                    in_x.rearrange("(c p) t -> p c t", p=128)[:, :, t0:t0 + TFF],
                    waits=w, inc=f"{tag}_x{sl3}")
            lna.stats(p, xstage[sl3], [t_xin])
            frontends[t] = t_xin

        bc_tok = {}

        def frontend2(t):
            sl3 = t % 3
            t_bc = bc_tok[t]
            def ap1(eng, sl3=sl3):
                return eng.tensor_mul(
                    out=f2scr[:, :], in0=xstage[sl3][:, :],
                    in1=rep_view(lna.bc, 0, TFF, NC))
            wap = [t_bc]
            if t >= 2:
                wap.append(tok_h_cons[t - 2])
            p.op("vector", ap1, waits=wap)
            def ap2(eng, t=t):
                return eng.tensor_add(
                    out=tgt[t % 2][:, :], in0=f2scr[:, :],
                    in1=rep_view(lna.bc, TFF, TFF, NC))
            wap2 = []
            if t >= 2 and tok_tgt_done[t - 2] is not None:
                wap2.append(tok_tgt_done[t - 2])
            t_tgt = p.op("vector", ap2, waits=wap2, inc="s_dve")
            lna.set_bc_user(t_tgt)
            t_tgt_tok[t] = t_tgt

        def backend_w1(t, flushes):
            t_tgt = t_tgt_tok[t]
            hrot = [ps_h[0], ps_h[1], ps_eo[0], ps_eo[1], ps_eo[2]]
            t_relu = [None] * 32
            for ft in range(32):
                hb = ft % 5
                wmm = [t_tgt, w1_tok[(NC - 1, ft // 8)]]
                if ft >= 5:
                    wmm.append(t_relu[ft - 5])
                if ft < 2 and t >= 1:
                    wmm.append(tok_h_cons[t - 1])
                if 2 <= ft < 5 and t >= 1:
                    wmm.append(tok_eo_cons.get((t - 1, [6, 7, 5][ft - 2])))
                def mmw1(eng, ft=ft, hb=hb, t=t):
                    r = None
                    for ec in range(NC):
                        r = eng.matmul(
                            hrot[hb][:, :],
                            W1b[:, ec * FF + ft * 128:ec * FF + ft * 128 + 128],
                            tgt[t % 2][:, ec * TFF:ec * TFF + TFF],
                            start=(ec == 0), stop=(ec == NC - 1))
                    return r
                tw1 = p.op("tensor", mmw1, waits=wmm, inc="s_te")
                def relu_fn(eng, ft=ft, hb=hb):
                    return eng.activation(
                        out=hsb[:, ft * TFF:ft * TFF + TFF],
                        in_=hrot[hb][:, :],
                        func=AF.Relu, bias=b1s[:, ft:ft + 1], scale=1.0)
                wrl = [tw1]
                if ft >= 16 and t + 1 < NTFF:
                    wrl.append(t_tgt_tok[t + 1])
                t_relu[ft] = p.op("scalar", relu_fn, waits=wrl, inc="s_act")
                if ft % 8 == 7 and flushes:
                    flushes.pop(0)()
            while flushes:
                flushes.pop(0)()
            tok_relu31[t] = t_relu[31]

        def tail_pieces(t):
            sl3 = t % 3
            t0 = t * TFF
            st = {}
            def p1():
                w = [tok_x2done[t]]
                if prev_lndma[0] is not None:
                    w.append(prev_lndma[0])
                lnb.stats(p, x2st, w)
            def p2():
                t_bc2 = lnb.bcast_in(p)
                def iap1(eng):
                    return eng.tensor_mul(
                        out=x2st[:, :], in0=x2st[:, :],
                        in1=rep_view(lnb.bc, 0, TFF, NC))
                p.op("vector", iap1, waits=[t_bc2, w2_all[0]])
                def iap2(eng):
                    return eng.tensor_add(
                        out=x2st[:, :], in0=x2st[:, :],
                        in1=rep_view(lnb.bc, TFF, TFF, NC))
                t_u = p.op("vector", iap2)
                t_fin_v = None
                t_fin_g = None
                for c in range(NC):
                    teng = "vector" if (t < 4 or c % 2 == 0) else "gpsimd"
                    def actg(eng, c=c):
                        return eng.tensor_scalar(
                            out=x2st[:, c * TFF:c * TFF + TFF],
                            in0=x2st[:, c * TFF:c * TFF + TFF],
                            scalar1=gh[:, c:c + 1], scalar2=bh[:, c:c + 1],
                            op0=ALU.mult, op1=ALU.add)
                    t_v = p.op(teng, actg, waits=[t_u])
                    def resid(eng, c=c, sl3=sl3):
                        return eng.tensor_add(
                            out=xstage[sl3][:, c * TFF:c * TFF + TFF],
                            in0=xstage[sl3][:, c * TFF:c * TFF + TFF],
                            in1=x2st[:, c * TFF:c * TFF + TFF])
                    tok = p.op(teng, resid, waits=[t_v])
                    if teng == "vector":
                        t_fin_v = tok
                    else:
                        t_fin_g = tok
                fins = [x for x in (t_fin_v, t_fin_g) if x is not None]
                pool_tok = t_fin_g if t_fin_g is not None else t_fin_v
                prev_uw[0] = pool_tok
                lnb.set_bc_user(pool_tok)
                prev_fin[0] = pool_tok
                t_x1_dma = p.dma(
                    "gpsimd",
                    out_x.rearrange("(c p) t -> p c t", p=128)[:, :, t0:t0 + TFF],
                    xstage[sl3].rearrange("p (c j) -> p c j", j=TFF),
                    waits=fins, inc=f"{tag}_ox{t % 2}")
                st["fins"] = fins
                st["x1"] = t_x1_dma
                if out_ln is None:
                    tok_tile_done[t] = [t_x1_dma]
            def p3():
                lnb.stats(p, xstage[sl3], st["fins"])
            def p4():
                t_bc3 = lnb.bcast_in(p)
                HW = NC * TFF // 2
                t_lno = None
                for h in range(2):
                    def lap1(eng, h=h, sl3=sl3):
                        return eng.tensor_mul(
                            out=sc[:, :], in0=xstage[sl3][:, h * HW:(h + 1) * HW],
                            in1=rep_view(lnb.bc, 0, TFF, NC // 2))
                    p.op("vector", lap1, waits=[t_bc3] if h == 0 else [])
                    def lap2(eng, h=h):
                        return eng.tensor_add(
                            out=xbB[:, h * HW:(h + 1) * HW], in0=sc[:, :],
                            in1=rep_view(lnb.bc, TFF, TFF, NC // 2))
                    t_lno = p.op("vector", lap2,
                                 inc="s_dve" if h == 1 else None)
                lnb.set_bc_user(t_lno)
                t_ln_dma = p.dma(
                    "gpsimd",
                    out_ln.rearrange("(c p) t -> p c t", p=128)[:, :, t0:t0 + TFF],
                    xbB.rearrange("p (c j) -> p c j", j=TFF),
                    waits=[t_lno], inc=f"{tag}_oln")
                prev_lndma[0] = t_ln_dma
                tok_tile_done[t] = [st["x1"], t_lno]
            if out_ln is not None:
                return [p1, p2, p3, p4]
            return [p1, p2]

        def backend_w2(t):
            t_w2 = [None] * NC
            t_evac = [None] * NC
            for dt in range(NC):
                wmm = [tok_relu31[t], w2_all[0], tok_eo_cons.get((t, dt - 3))]
                if dt < 3 and t >= 1:
                    wmm.append(tok_eo_cons.get((t - 1, dt + NC - 3)))
                def mmw2(eng, dt=dt):
                    r = None
                    for fc in range(32):
                        r = eng.matmul(
                            ps_eo[dt % 3][:, :],
                            W2b[:, fc * D + dt * 128:fc * D + dt * 128 + 128],
                            hsb[:, fc * TFF:fc * TFF + TFF],
                            start=(fc == 0), stop=(fc == 31))
                    return r
                t_w2[dt] = p.op("tensor", mmw2, waits=wmm, inc="s_te")
                def evac(eng, dt=dt, t=t):
                    return eng.scalar_tensor_tensor(
                        out=x2st[:, dt * TFF:dt * TFF + TFF],
                        in0=ps_eo[dt % 3][:, :],
                        scalar=b2s[:, dt:dt + 1],
                        in1=tgt[t % 2][:, dt * TFF:dt * TFF + TFF],
                        op0=ALU.add, op1=ALU.add)
                wev = [t_w2[dt], w2_all[0]]
                if t >= 1 and dt == 0:
                    wev.append(prev_fin[0])
                    wev.append(prev_uw[0])
                t_evac[dt] = p.op("vector", evac, waits=wev, inc="s_dve")
                tok_eo_cons[(t, dt)] = t_evac[dt]
            tok_x2done[t] = t_evac[NC - 1]
            tok_h_cons[t] = t_w2[NC - 1]
            tok_tgt_done[t] = t_evac[NC - 1]

        tok_relu31 = {}
        tok_x2done = {}
        prev_lndma = [None]
        for it in range(NTFF + 3):
            if 1 <= it <= NTFF:
                frontend2(it - 1)
            pieces = tail_pieces(it - 3) if it >= 3 else []
            if 2 <= it <= NTFF + 1:
                backend_w1(it - 2, pieces)
            else:
                while pieces:
                    pieces.pop(0)()
            if 2 <= it <= NTFF + 1:
                backend_w2(it - 2)
            if it < NTFF:
                frontend1(it)
                bc_tok[it] = lna.bcast_in(p)
        es.close()

    def b_phase(t_start):
        es = contextlib.ExitStack()
        sbl = lambda *a: es.enter_context(nc.sbuf_tensor(*a))
        Kres = [sbl(f"Kr{i}", [128, S], BF16) for i in range(NPAIR)]
        Vres = sbl("Vr", [128, NKC * H * (DH + 1)], BF16)
        t_last = b1_sub(es, t_start, Kres, Vres)
        p.barrier()
        t_last = b2_sub(es, t_last, Kres, Vres)
        es.close()
        return t_last

    def b1_sub(es_outer, t_start, Kres, Vres):
        es = contextlib.ExitStack()
        sbl = lambda *a: es.enter_context(nc.sbuf_tensor(*a))
        ptl = lambda *a: es.enter_context(nc.psum_tensor(*a))
        bvs = sbl("bvs", [128, D], F32)
        tbv = p.dma("sync", bvs[:, :],
                    bass.AP(bvF, 0, [[0, 128], [1, D]]),
                    waits=[t_start], inc="s_prm")
        Wqb = sbl("B1Wqb", [128, NC * D], BF16)
        Wkb = sbl("B1Wkb", [128, NC * D], BF16)
        Wvb = sbl("B1Wvb", [128, NC * D], BF16)
        wst = sbl("B1wst", [128, 2 * 512], F32)
        wst2 = nc.alloc_sbuf_tensor_at(
            "B1wst2", [128, 2 * 512], F32, offset=nc.lookup_mloc(Vres).addr)
        lnst = [sbl(f"B1ln{i}", [128, NC * TB], BF16) for i in range(2)]
        qst = sbl("B1q", [128, 2 * TB], BF16)
        sqs = sbl("B1sq", [128, 2 * TB], BF16)
        ps_qk = [ptl(f"B1pqk{i}", [128, TB], F32) for i in range(4)]
        ps_v = [ptl(f"B1pv{i}", [128, TB], F32) for i in range(2)]
        ps_n = [ptl(f"B1pn{i}", [1, TB], F32) for i in range(2)]

        lanes = {
            "sync": {"wst": wst, "ceng": "scalar", "inc": "s_act",
                     "pend": [], "deferred": [], "n": 0},
            "gpsimd": {"wst": wst2, "ceng": "gpsimd", "inc": "s_gp",
                       "pend": [], "deferred": [], "n": 0},
        }
        wlast = {}
        def flush_one(q):
            L = lanes[q]
            cfn, tdma, wid = L["deferred"].pop(0)
            tcast = p.op(L["ceng"], cfn, waits=[tdma], inc=L["inc"])
            L["pend"].append(tcast)
            wlast.setdefault(wid, {})[q] = tcast
            return tcast
        def stream(q, dst, dsto, src, wid):
            L = lanes[q]
            i = L["n"]
            L["n"] += 1
            slot = i % 2
            w = [t_start] if i == 0 else []
            if i >= 2:
                w.append(L["pend"][i - 2])
            tdma = p.dma(q, L["wst"][:, slot * 512:slot * 512 + 512], src,
                         waits=w, inc=f"B1w{q[0]}{slot}")
            if L["ceng"] == "scalar":
                def cfn(eng, dst=dst, dsto=dsto, slot=slot, L=L):
                    return eng.activation(
                        out=dst[:, dsto:dsto + 512],
                        in_=L["wst"][:, slot * 512:slot * 512 + 512],
                        func=AF.Copy, bias=0.0, scale=1.0)
            else:
                def cfn(eng, dst=dst, dsto=dsto, slot=slot, L=L):
                    return eng.tensor_copy(out=dst[:, dsto:dsto + 512],
                                           in_=L["wst"][:, slot * 512:slot * 512 + 512])
            L["deferred"].append((cfn, tdma, wid))
            if len(L["deferred"]) > 1:
                flush_one(q)
        pre_ln = {}
        for t in range(2):
            pre_ln[t] = p.dma(
                "scalar", lnst[t].rearrange("p (c j) -> p c j", j=TB),
                ln1T.rearrange("(c p) t -> p c t", p=128)[:, :, t * TB:(t + 1) * TB],
                waits=[t_start], inc=f"B1ln{t}")
        ci = 0
        for wid, (Wb, WT) in enumerate(((Wqb, WqT), (Wkb, WkT), (Wvb, WvT))):
            for ec in range(NC):
                for db in range(D // 512):
                    q = "sync" if ci % 2 == 0 else "gpsimd"
                    ci += 1
                    stream(q, Wb, ec * D + db * 512,
                           WT[ec * 128:(ec + 1) * 128, db * 512:(db + 1) * 512],
                           wid)
        for q in lanes:
            while lanes[q]["deferred"]:
                flush_one(q)
        wq_all = list(wlast[0].values())
        wk_all = list(wlast[1].values())
        wv_all = list(wlast[2].values())

        def vones(eng):
            return eng.memset(
                Vres[:, :].rearrange("p (k h d) -> p k h d", h=H, d=DH + 1)
                [:, :, :, DH:DH + 1], 1.0)
        t_vones = p.op("gpsimd", vones, waits=list(wlast[2].values()), inc="s_gp")


        tok_ln_done = [None] * NTB
        t_qst_free = [None, None]
        rot_free = [None, None, None, None]
        vrot_free = [None, None]
        nfree = [None, None]
        sq_free = [None, None]
        t_ev = None

        for t in range(NTB):
            sl = t % 2
            t0 = t * TB
            if t < 2:
                t_ln = pre_ln[t]
            else:
                t_ln = p.dma(
                    "sync", lnst[sl].rearrange("p (c j) -> p c j", j=TB),
                    ln1T.rearrange("(c p) t -> p c t", p=128)[:, :, t0:t0 + TB],
                    waits=list(tok_ln_done[t - 2]), inc=f"B1ln{sl}")
            ln_users = []
            pending_norm = []
            for kind in range(2):
                Wb = Wqb if kind == 0 else Wkb
                bias = bqs if kind == 0 else bks
                m2 = qmax2 if kind == 0 else kmax2
                for pr in range(NPAIR):
                    if len(pending_norm) > 1:
                        pending_norm.pop(0)()
                    rot = (kind * NPAIR + pr) % 4
                    wmm = [t_ln, rot_free[rot]]
                    if t == 0 and pr == 0:
                        wmm.extend(wq_all if kind == 0 else wk_all)
                    def mmqk(eng, pr=pr, sl=sl, rot=rot, Wb=Wb):
                        r = None
                        for ec in range(NC):
                            r = eng.matmul(
                                ps_qk[rot][:, :],
                                Wb[:, ec * D + pr * 128:ec * D + pr * 128 + 128],
                                lnst[sl][:, ec * TB:ec * TB + TB],
                                start=(ec == 0), stop=(ec == NC - 1))
                        return r
                    t_mm = p.op("tensor", mmqk, waits=wmm, inc="s_te")
                    if kind == 0:
                        wev = [t_mm] + (t_qst_free[sl] or [])
                        dst = qst[:, sl * TB:sl * TB + TB]
                    else:
                        wev = [t_mm]
                        dst = Kres[pr][:, t0:t0 + TB]
                    def ev(eng, dst=dst, rot=rot, pr=pr, bias=bias):
                        return eng.tensor_scalar_add(dst, ps_qk[rot][:, :],
                                                     bias[:, pr:pr + 1])
                    t_ev = p.op("vector", ev, waits=wev, inc="s_dve")
                    rot_free[rot] = t_ev
                    ln_users.append(t_mm)
                    if kind == 0:
                        t_qdma = p.dma(
                            "gpsimd", QsT[pr * 128:(pr + 1) * 128, t0:t0 + TB],
                            qst[:, sl * TB:sl * TB + TB],
                            waits=[t_ev], inc=f"B1q{sl}")
                    sqb = (pr % 2) * TB
                    wsq = [t_ev] + (sq_free[pr % 2] or [])
                    def sqf(eng, dst=dst, sqb=sqb):
                        return eng.tensor_mul(out=sqs[:, sqb:sqb + TB],
                                              in0=dst, in1=dst)
                    t_sq = p.op("vector", sqf, waits=wsq, inc="s_dve")
                    if kind == 0:
                        t_qst_free[sl] = [t_qdma, t_sq]
                    def norm_thunk(pr=pr, sqb=sqb, t_sq=t_sq, m2=m2):
                        t_nm = [None, None]
                        for sub in range(2):
                            wn = [t_sq, nfree[sub]]
                            def nmm(eng, sqb=sqb, sub=sub):
                                return eng.matmul(
                                    ps_n[sub][:, :],
                                    ones_col_b[sub * 64:sub * 64 + 64, :],
                                    sqs[sub * 64:sub * 64 + 64, sqb:sqb + TB],
                                    start=True, stop=True,
                                    tile_position=(sub * 64, 0))
                            t_nm[sub] = p.op("tensor", nmm, waits=wn, inc="s_te")
                        sq_free[pr % 2] = t_nm
                        for sub in range(2):
                            hh = 2 * pr + sub
                            def red1(eng, sub=sub):
                                return eng.reduce_max(mred[0:1, sub:sub + 1],
                                                      ps_n[sub][:, :],
                                                      axis=mybir.AxisListType.X)
                            t_r1 = p.op("vector", red1, waits=[t_nm[sub]],
                                        inc="s_dve")
                            nfree[sub] = t_r1
                            def red2(eng, hh=hh, sub=sub, m2=m2):
                                return eng.tensor_max(m2[0:1, hh:hh + 1],
                                                      m2[0:1, hh:hh + 1],
                                                      mred[0:1, sub:sub + 1])
                            p.op("vector", red2, inc="s_dve")
                    pending_norm.append(norm_thunk)
            while pending_norm:
                pending_norm.pop(0)()
            for kci in range(TB // 128):
                kc = t * (TB // 128) + kci
                j0 = kci * 128
                for half in range(2):
                    vr = (kci * 2 + half) % 2
                    wmm = [t_ln, vrot_free[vr], t_vones if t == 0 else None,
                           tbv if t == 0 else None]
                    if t == 0:
                        wmm.extend(wv_all)
                    def mmv(eng, kc=kc, j0=j0, half=half, sl=sl, vr=vr):
                        r = None
                        for ec in range(NC):
                            r = eng.matmul(
                                ps_v[vr][:, :],
                                lnst[sl][:, ec * TB + j0:ec * TB + j0 + 128],
                                Wvb[:, ec * D + half * 512:ec * D + half * 512 + 512],
                                start=(ec == 0), stop=(ec == NC - 1))
                        return r
                    t_vmm = p.op("tensor", mmv, waits=wmm, inc="s_te")
                    ln_users.append(t_vmm)
                    def vev(eng, kc=kc, half=half, vr=vr):
                        return eng.tensor_add(
                            out=Vres[:, kc * 1040 + half * 8 * 65:]
                            .rearrange("p (h d) -> p h d", d=65)[:, 0:8, 0:64],
                            in0=ps_v[vr][:, :].rearrange("p (h d) -> p h d", d=64),
                            in1=bvs[:, half * 512:half * 512 + 512]
                            .rearrange("p (h d) -> p h d", d=64))
                    t_vev = p.op("vector", vev, waits=[t_vmm], inc="s_dve")
                    vrot_free[vr] = t_vev
            tok_ln_done[t] = [ln_users[-1]]
        es.close()
        return t_ev

    def b2_sub(es_outer, t_start, Kres, Vres):
        es = contextlib.ExitStack()
        sbl = lambda *a: es.enter_context(nc.sbuf_tensor(*a))
        ptl = lambda *a: es.enter_context(nc.psum_tensor(*a))
        Wob = sbl("B2Wob", [128, NC * D], BF16)
        wst = sbl("B2wst", [128, 2 * 512], F32)
        x1st = sbl("B2x1", [128, NC * TB], F32)
        x2st = x1st
        Oqb = sbl("B2O", [128, NPAIR * TB], BF16)
        Otmp = sbl("B2Ot", [64, TB], BF16)
        Rsb = sbl("B2R", [64, TB], F32)
        Rsb2 = sbl("B2R2", [64, TB], F32)
        rr2 = sbl("B2rr", [128, TB], F32)
        ones64 = sbl("B2one", [128, 64], F32)
        Qst = [sbl(f"B2q{i}", [128, TB], BF16) for i in range(2)]
        pT = [sbl(f"B2p{i}", [128, 2048], BF16) for i in range(6)]
        SR = ptl("B2sr", [128, 3072], F32)
        ps_o = [ptl("B2o1", [128, TB], F32), ptl("B2o2", [128, TB], F32)]

        rot_free = [None, None, None]
        rot_n = [0]
        def onesf(eng):
            eng.memset(ones64[:, :], 1.0)
            return eng.memset(rr2[:, :], 1.0)
        t_ones64 = p.op("vector", onesf, waits=[t_start], inc="s_dve")
        def take_slot():
            r = rot_n[0] % 3
            rot_n[0] += 1
            return r

        def st1(eng):
            return eng.activation(out=cln[0:1, 0:H], in_=qmax2[0:1, :],
                                  func=AF.Ln, bias=c_zero[0:1, 0:1], scale=1.0)
        t1 = p.op("scalar", st1, waits=[t_start], inc="s_act")
        def st2(eng):
            return eng.activation(out=cln[0:1, H:2 * H], in_=kmax2[0:1, :],
                                  func=AF.Ln, bias=c_zero[0:1, 0:1], scale=1.0)
        t2 = p.op("scalar", st2, inc="s_act")
        def st3(eng):
            return eng.tensor_add(negc[0:1, :], cln[0:1, 0:H], cln[0:1, H:2 * H])
        t3 = p.op("vector", st3, waits=[t2], inc="s_dve")
        def st4(eng):
            return eng.activation(out=negc[0:1, :], in_=negc[0:1, :],
                                  func=AF.Exp, bias=c_log8[0:1, 0:1], scale=0.5)
        t4 = p.op("scalar", st4, waits=[t3], inc="s_act")
        def st4b(eng):
            return eng.reduce_max(negc[0:1, 16:17] if False else cln[0:1, 0:1],
                                  negc[0:1, :], axis=mybir.AxisListType.X)
        t4b = p.op("vector", st4b, waits=[t4], inc="s_dve")
        def st5(eng):
            return eng.tensor_scalar_mul(cln[0:1, 0:1], cln[0:1, 0:1], -1.0)
        t5 = p.op("vector", st5, inc="s_dve")
        def st6(eng):
            return eng.matmul(SR[:, 0:1], ones_row_f[0:1, 0:128],
                              cln[0:1, 0:1], start=True, stop=True)
        t6 = p.op("tensor", st6, waits=[t5], inc="s_te")
        def st7(eng):
            return eng.copy(out=cbias[:, 0:1], in_=SR[:, 0:1])
        t_cb = p.op("scalar", st7, waits=[t6], inc="s_act")
        rot_free[0] = t_cb

        pend = [t_start, t_start]
        for jc in range(NC):
            for db in range(D // 512):
                slot = len(pend) % 2
                w = [pend[-2]]
                tdma = p.dma("scalar", wst[:, slot * 512:slot * 512 + 512],
                             WoT[jc * 128:(jc + 1) * 128, db * 512:(db + 1) * 512],
                             waits=w, inc=f"B2w{slot}")
                def cfn(eng, jc=jc, db=db, slot=slot):
                    return eng.tensor_copy(
                        out=Wob[:, jc * D + db * 512:jc * D + db * 512 + 512],
                        in_=wst[:, slot * 512:slot * 512 + 512])
                tcast = p.op("gpsimd", cfn, waits=[tdma], inc="s_gp")
                pend.append(tcast)
        wo_all = pend[-1]

        NG = 16
        tok_qb_done = [None] * NTB
        q_free = [None, None]
        o_free = [None, None]
        R_free = [None, None]
        rrow_free = [None]
        Rdram_free = [None, None]
        tail_par = 0
        Otmp_free = None
        t_x2 = None

        t_oshift = None
        t_n1 = None
        pending_avs = []
        pending_tail = [None]
        pending_tail2 = [None]
        pending_out = []
        last_x2 = [None]
        for qb in range(NTB):
            q0 = qb * TB
            while pending_avs:
                pending_avs.pop(0)()
            tq_pre = {}
            for pr in (0, 1):
                wq = [q_free[pr % 2]] if q_free[pr % 2] else []
                if qb == 0 and pr == 0:
                    wq.append(t_cb)
                tq_pre[pr] = p.dma("sync", Qst[pr % 2][:, :],
                                   QsT[pr * 128:(pr + 1) * 128, q0:q0 + TB],
                                   waits=wq, inc=f"B2qd{pr % 2}")
            t_x1 = None
            tail_tok = {}
            for pr in range(NPAIR):
                poff = (4 * (qb * NPAIR + pr)) % 6
                if pr == 1:
                    wx = [wo_all if qb == 0 else None]
                    if qb >= 1 and tok_qb_done[qb - 1] is not None:
                        wx.extend(tok_qb_done[qb - 1])
                    t_x1 = p.dma(
                        "sync", x1st.rearrange("p (c j) -> p c j", j=TB),
                        x1T.rearrange("(c p) t -> p c t", p=128)[:, :, q0:q0 + TB],
                        waits=wx, inc="B2x1d")
                sl = pr % 2
                if pr in tq_pre:
                    t_q = tq_pre[pr]
                else:
                    wq = [q_free[sl]] if q_free[sl] else []
                    t_q = p.dma("sync", Qst[sl][:, :],
                                QsT[pr * 128:(pr + 1) * 128, q0:q0 + TB],
                                waits=wq, inc=f"B2qd{sl}")
                t_exp = {}
                t_av = {}
                def emit_av(g, t_exp=t_exp, t_av=t_av, pr=pr, poff=poff):
                    psl = (g + poff) % 6
                    wav = [t_exp[g]]
                    if g == 0:
                        wav.extend(o_free)
                    def mmav(eng, g=g, pr=pr, psl=psl):
                        rr = None
                        for sub in range(2):
                            for i in range(2):
                                kc = g * 2 + i
                                rr = eng.matmul(
                                    ps_o[sub][0:65, :],
                                    Vres[:, kc * 1040 + (2 * pr + sub) * 65:
                                         kc * 1040 + (2 * pr + sub) * 65 + 65],
                                    pT[psl][:, sub * 1024 + i * TB:
                                            sub * 1024 + i * TB + TB],
                                    start=(kc == 0), stop=(kc == NKC - 1))
                        return rr
                    t_av[g] = p.op("tensor", mmav, waits=wav, inc="s_te")
                for g in range(NG):
                    if g == 1:
                        while pending_avs:
                            pending_avs.pop(0)()
                        if pending_tail[0] is not None:
                            pending_tail[0]()
                            pending_tail[0] = None
                    r0 = take_slot()
                    r1 = take_slot()
                    psl = (g + poff) % 6
                    wmm = [rot_free[r0], rot_free[r1]]
                    if g == 0:
                        wmm.append(t_q)
                    def mmsc0(eng, g=g, pr=pr, r0=r0):
                        rr = None
                        for i in range(2):
                            kc = g * 2 + i
                            rr = eng.matmul(
                                SR[:, r0 * 1024 + i * TB:r0 * 1024 + i * TB + TB],
                                Kres[pr][0:64, kc * 128:kc * 128 + 128],
                                Qst[pr % 2][0:64, :],
                                start=True, stop=True, tile_position=(0, 0))
                        return rr
                    t_sc0 = p.op("tensor", mmsc0, waits=wmm, inc="s_te")
                    def mmsc1(eng, g=g, pr=pr, r1=r1):
                        rr = None
                        for i in range(2):
                            kc = g * 2 + i
                            rr = eng.matmul(
                                SR[:, r1 * 1024 + i * TB:r1 * 1024 + i * TB + TB],
                                Kres[pr][64:128, kc * 128:kc * 128 + 128],
                                Qst[pr % 2][64:128, :],
                                start=True, stop=True, tile_position=(64, 0))
                        return rr
                    t_sc = p.op("tensor", mmsc1, inc="s_te")
                    wexp = [t_sc0, t_av.get(g - 6)]
                    def expf0(eng, psl=psl, r0=r0):
                        return eng.activation(
                            out=pT[psl][:, 0:1024],
                            in_=SR[:, r0 * 1024:r0 * 1024 + 1024],
                            func=AF.Exp, bias=cbias[:, 0:1], scale=0.125)
                    t_e0 = p.op("scalar", expf0, waits=wexp, inc="s_act")
                    rot_free[r0] = t_e0
                    def expf1(eng, psl=psl, r1=r1):
                        return eng.activation(
                            out=pT[psl][:, 1024:2048],
                            in_=SR[:, r1 * 1024:r1 * 1024 + 1024],
                            func=AF.Exp, bias=cbias[:, 0:1], scale=0.125)
                    t_e = p.op("scalar", expf1, waits=[t_sc], inc="s_act")
                    t_exp[g] = t_e
                    rot_free[r1] = t_e
                    if g == 3 and pending_tail2[0] is not None:
                        pending_tail2[0]()
                        pending_tail2[0] = None
                    if g >= 4 and pending_out:
                        pending_out.pop(0)()
                    if g >= 5:
                        emit_av(g - 5)
                emit_av(NG - 5)
                emit_av(NG - 4)
                emit_av(NG - 3)
                def av_tail(g, sl=sl, t_av=t_av, emit_av=emit_av):
                    def thunk():
                        emit_av(g)
                        if g == NG - 1:
                            q_free[sl] = t_av[NG - 1]
                    return thunk
                pending_avs.append(av_tail(NG - 2))
                pending_avs.append(av_tail(NG - 1))

                rc_hold = {}
                def emit_tail1(pr=pr, t_av=t_av, rc_hold=rc_hold):
                    t_avl = t_av[NG - 1]
                    def lc1(eng):
                        return eng.tensor_copy(out=rr2[0:1, :],
                                               in_=ps_o[0][64:65, :])
                    p.op("vector", lc1, waits=[t_avl, R_free[0], t_ones64])
                    def lc2(eng):
                        return eng.tensor_copy(out=rr2[64:65, :],
                                               in_=ps_o[1][64:65, :])
                    p.op("vector", lc2, waits=[t_avl, R_free[1]])
                    def rc(eng):
                        return eng.reciprocal(rr2[:, :], rr2[:, :])
                    t_rc = p.op("vector", rc, inc="s_dve")
                    rc_hold["rc"] = (t_rc, t_rc)
                def emit_tail2(pr=pr, tail_tok=tail_tok, rc_hold=rc_hold):
                    nonlocal R_free, Otmp_free, t_oshift, t_n1
                    t_rc1, t_rc2 = rc_hold["rc"]
                    r = take_slot()
                    def bc12(eng, r=r):
                        eng.matmul(SR[0:64, r * 1024:r * 1024 + TB],
                                   ones64[0:1, :],
                                   rr2[0:1, :], start=True, stop=True)
                        return eng.matmul(SR[0:64, r * 1024 + TB:r * 1024 + 2 * TB],
                                          ones64[64:65, :],
                                          rr2[64:65, :], start=True, stop=True,
                                          tile_position=(64, 0))
                    t_b12 = p.op("tensor", bc12, waits=[t_rc1, t_rc2, rot_free[r]],
                                 inc="s_te")
                    def bc1e(eng, r=r):
                        return eng.tensor_copy(out=Rsb[0:64, :],
                                               in_=SR[0:64, r * 1024:r * 1024 + TB])
                    t_b1e = p.op("vector", bc1e, waits=[t_b12, R_free[0]],
                                 inc="s_dve")
                    def bc2e(eng, r=r):
                        return eng.tensor_copy(out=Rsb2[0:64, :],
                                               in_=SR[0:64, r * 1024 + TB:r * 1024 + 2 * TB])
                    t_b2e = p.op("vector", bc2e, waits=[R_free[1]], inc="s_dve")
                    rot_free[r] = t_b2e
                    def nm1(eng, pr=pr):
                        return eng.tensor_mul(out=Oqb[0:64, pr * TB:pr * TB + TB],
                                              in0=ps_o[0][0:64, :], in1=Rsb[0:64, :])
                    t_n1 = p.op("vector", nm1, waits=[t_b1e], inc="s_dve")
                    o_free[0] = t_n1
                    wn2 = [t_b2e, Otmp_free]
                    def nm2(eng):
                        return eng.tensor_mul(out=Otmp[0:64, :],
                                              in0=ps_o[1][0:64, :], in1=Rsb2[0:64, :])
                    t_n2 = p.op("vector", nm2, waits=wn2, inc="s_dve")
                    o_free[1] = t_n2
                    R_free = [t_n1, t_n2]
                    t_sh = p.dma("gpsimd", Oqb[64:128, pr * TB:pr * TB + TB],
                                 Otmp[0:64, :], waits=[t_n2], inc="B2sh")
                    Otmp_free = t_sh
                    t_oshift = t_sh
                    tail_tok[pr] = (t_n1, t_sh)
                pending_tail[0] = emit_tail1
                pending_tail2[0] = emit_tail2
            ost = {"last": None}
            def mk_out(it, tail_tok=tail_tok, t_x1=t_x1):
                def thunk(it=it, tail_tok=tail_tok, t_x1=t_x1):
                    r = take_slot()
                    wmm = [tail_tok[NC - 2][0], tail_tok[NC - 2][1], rot_free[r]]
                    def mmo_a(eng, it=it, r=r):
                        rr = None
                        for jc in range(NC - 1):
                            rr = eng.matmul(
                                SR[:, r * 1024:r * 1024 + TB],
                                Wob[:, jc * D + it * 128:jc * D + it * 128 + 128],
                                Oqb[:, jc * TB:jc * TB + TB],
                                start=(jc == 0), stop=False)
                        return rr
                    p.op("tensor", mmo_a, waits=wmm)
                    def mmo_b(eng, it=it, r=r):
                        jc = NC - 1
                        return eng.matmul(
                            SR[:, r * 1024:r * 1024 + TB],
                            Wob[:, jc * D + it * 128:jc * D + it * 128 + 128],
                            Oqb[:, jc * TB:jc * TB + TB],
                            start=False, stop=True)
                    t_mo = p.op("tensor", mmo_b,
                                waits=[tail_tok[NC - 1][0], tail_tok[NC - 1][1]],
                                inc="s_te")
                    def evo(eng, it=it, r=r):
                        return eng.scalar_tensor_tensor(
                            out=x2st[:, it * TB:it * TB + TB],
                            in0=SR[:, r * 1024:r * 1024 + TB],
                            scalar=bos[:, it:it + 1],
                            in1=x1st[:, it * TB:it * TB + TB],
                            op0=ALU.add, op1=ALU.add)
                    t_ev = p.op("vector", evo, waits=[t_mo, t_x1], inc="s_dve")
                    rot_free[r] = t_ev
                    ost["last"] = t_ev
                return thunk
            def mk_x2(qb=qb, q0=q0, ost=ost):
                def thunk():
                    t_x2 = p.dma(
                        "gpsimd",
                        x2T.rearrange("(c p) t -> p c t", p=128)[:, :, q0:q0 + TB],
                        x2st.rearrange("p (c j) -> p c j", j=TB),
                        waits=[ost["last"]], inc="B2x2d")
                    tok_qb_done[qb] = [t_x2]
                    last_x2[0] = t_x2
                return thunk
            for it in range(NC):
                pending_out.append(mk_out(it))
            pending_out.append(mk_x2())
        while pending_avs:
            pending_avs.pop(0)()
        if pending_tail[0] is not None:
            pending_tail[0]()
            pending_tail[0] = None
        if pending_tail2[0] is not None:
            pending_tail2[0]()
            pending_tail2[0] = None
        while pending_out:
            pending_out.pop(0)()
        es.close()
        return last_x2[0]

    t0tok = t_init
    if "A" in phases:
        ffn_phase("A", xT, None, x1T, ln1T, t0tok, save_bf16=True)
        p.barrier()
    if "B" in phases:
        b_phase(t0tok)
        p.barrier()
    if "C" in phases:
        ffn_phase("C", x2T, None, yT, None, t0tok, bf16_src=True)
        p.barrier()
    p.barrier()

    p.emit()
    ctx.close()
    return nc



_NC_CACHE = {}


def _get_nc():
    if "nc" not in _NC_CACHE:
        _NC_CACHE["nc"] = build(phases="ABC", dev=False)
    return _NC_CACHE["nc"]


def kernel(L0_attn_hidden_state, L0_attn_Wq, L0_attn_bq, L0_attn_Wk, L0_attn_bk,
           L0_attn_Wv, L0_attn_bv, L0_attn_Wo, L0_attn_bo, L0_ffn_router_w,
           L0_ffn_router_b, L0_ffn_w1, L0_ffn_b1, L0_ffn_w2, L0_ffn_b2,
           L0_ffn_gamma, L0_ffn_beta):
    from concourse.bass_utils import run_bass_kernel_spmd
    x = np.asarray(L0_attn_hidden_state)
    B = x.shape[0]
    asc = np.ascontiguousarray
    shared = {
        "W1": asc(np.asarray(L0_ffn_w1)[0]),
        "W2": asc(np.asarray(L0_ffn_w2)[0]),
        "b1d": asc(np.asarray(L0_ffn_b1)[0].reshape(32, 128).T),
        "b2d": asc(np.asarray(L0_ffn_b2)[0].reshape(8, 128).T),
        "gammad": asc(np.asarray(L0_ffn_gamma).reshape(8, 128).T),
        "betad": asc(np.asarray(L0_ffn_beta).reshape(8, 128).T),
        "WqT": asc(np.asarray(L0_attn_Wq).transpose(2, 0, 1).reshape(1024, 1024)),
        "WkT": asc(np.asarray(L0_attn_Wk).transpose(2, 0, 1).reshape(1024, 1024)),
        "WvT": asc(np.asarray(L0_attn_Wv).transpose(2, 0, 1).reshape(1024, 1024)),
        "bqd": asc(np.asarray(L0_attn_bq).reshape(8, 128).T),
        "bkd": asc(np.asarray(L0_attn_bk).reshape(8, 128).T),
        "bvF": asc(np.asarray(L0_attn_bv).reshape(1, 1024)),
        "WoT": asc(np.asarray(L0_attn_Wo).T),
        "bod": asc(np.asarray(L0_attn_bo).reshape(8, 128).T),
    }
    in_maps = []
    for b in range(B):
        m = dict(shared)
        m["xT"] = asc(x[b].T)
        in_maps.append(m)
    nc = _get_nc()
    res = run_bass_kernel_spmd(nc, in_maps, core_ids=list(range(B)))
    out = np.empty((B, S, D), np.float32)
    for b in range(B):
        out[b] = res.results[b]["yT"].T
    return out


# version 29
# speedup vs baseline: 1.0164x; 1.0164x over previous
import contextlib
import numpy as np
import concourse.bass as bass
import concourse.mybir as mybir

F32 = mybir.dt.float32
BF16 = mybir.dt.bfloat16
AF = mybir.ActivationFunctionType
ALU = mybir.AluOpType

D = 1024
S = 4096
H = 16
DH = 64
FF = 4096
NC = D // 128
NPAIR = H // 2
TFF = 256
NTFF = S // TFF
TB = 512
NTB = S // TB
NKC = S // 128
LN_EPS = 1e-5
LOG_EIGHTH = float(np.log(0.125))


def rep_view(t, col0, T, n):
    W = t.shape[1]
    return bass.AP(t, col0, [[W, 128], [0, n], [1, T]])

ENGINES = ("sync", "tensor", "vector", "scalar", "gpsimd")


class Plan:

    def __init__(self, nc, ctx):
        self.nc = nc
        self.ctx = ctx
        self.entries = {e: [] for e in ENGINES}
        self.sems = {}
        self.counts = {}
        for s in ("s_te", "s_act", "s_dve", "s_gp"):
            self._mksem(s)

    def _mksem(self, s):
        if s not in self.sems:
            self.sems[s] = self.ctx.enter_context(self.nc.semaphore(s))
            self.counts[s] = 0

    _CHAIN = {"vector": "s_dve", "scalar": "s_act", "gpsimd": "s_gp"}

    def op(self, engine, fn, waits=(), inc=None, amt=1):
        flat = []
        for w in waits:
            if w is None:
                continue
            if isinstance(w, list):
                flat.extend(x for x in w if x is not None)
            else:
                flat.append(w)
        waits = flat
        if engine in self._CHAIN and fn is not None:
            s = self._CHAIN[engine]
            if inc is None:
                inc = s
            if self.counts.get(s, 0) > 0:
                waits.append((s, self.counts[s]))
        tok = None
        if inc is not None:
            self._mksem(inc)
            self.counts[inc] += amt
            tok = (inc, self.counts[inc])
        self.entries[engine].append((fn, waits, inc, amt))
        return tok

    def dma(self, engine, out, in_, waits=(), inc=None):
        assert inc is not None
        def fn(eng, out=out, in_=in_):
            return eng.dma_start(out=out, in_=in_)
        return self.op(engine, fn, waits=waits, inc=inc, amt=16)

    def wait(self, engine, waits):
        self.entries[engine].append((None, [w for w in waits if w is not None],
                                     None, 0))

    def barrier(self):
        toks = [(s, self.counts[s]) for s in self.sems if self.counts[s] > 0]
        for e in ENGINES:
            self.wait(e, toks)

    def emit(self):
        nc = self.nc
        with nc.Block() as block:
            def mk(name):
                chain = self._CHAIN.get(name)
                def run(eng):
                    hwm = {}
                    mirror = 0
                    for fn, waits, inc, amt in self.entries[name]:
                        for (sname, val) in waits:
                            if hwm.get(sname, -1) >= val:
                                continue
                            hwm[sname] = val
                            eng.wait_ge(self.sems[sname], val)
                        if fn is None:
                            continue
                        inst = fn(eng)
                        if inc is not None:
                            inst.then_inc(self.sems[inc], amt)
                            if inc == chain:
                                mirror += amt
                return run
            block.sync(mk("sync"))
            block.tensor(mk("tensor"))
            block.vector(mk("vector"))
            block.scalar(mk("scalar"))
            block.gpsimd(mk("gpsimd"))


class LNEmitter:

    def __init__(self, p, nc, ctx, tag, T, shared, xb=None, ceng="vector",
                 bcmode="dma", ones_row_f=None, sq_eng=None):
        self.p = p
        self.T = T
        self.tag = tag
        self.ceng = ceng
        self.sq_eng = sq_eng or ceng
        self.bcmode = bcmode
        self.ones_row_f = ones_row_f
        (self.ones_col_b, self.c_eps, self.c_zero) = shared
        sb = lambda *a: ctx.enter_context(nc.sbuf_tensor(*a))
        pt = lambda *a: ctx.enter_context(nc.psum_tensor(*a))
        self.xb = xb if xb is not None else sb(f"{tag}_lnxb", [128, NC * T], BF16)
        self.r_mu = sb(f"{tag}_rmu", [1, T], F32)
        self.r_var = sb(f"{tag}_rvar", [1, T], F32)
        self.r_ab = sb(f"{tag}_rab", [1, 2 * T], F32)
        self.ps_st = pt(f"{tag}_pst", [1, 2 * T], F32)
        if bcmode == "psum":
            self.bc = pt(f"{tag}_bc", [128, 2 * T], F32)
        else:
            self.bc = sb(f"{tag}_bc", [128, 2 * T], F32)
        self.dram = nc.dram_tensor(f"{tag}_rows", [2, 2 * T], F32, kind="Internal")
        self.xb_free = None
        self.ps_free = None
        self.rows_free = None
        self.dram_free = [None, None]
        self.bc_user = None
        self.inv = 0
        self.pending = []

    def stats_cast(self, p, xsrc, deps):
        w = list(deps) + [self.xb_free]
        def cast_fn(eng):
            return eng.tensor_copy(out=self.xb[:, :NC * self.T],
                                   in_=xsrc[:, :NC * self.T])
        return p.op(self.ceng, cast_fn, waits=w)

    def stats(self, p, xsrc, deps, precast=None):
        T = self.T
        if precast is not None:
            t_cast = precast
        else:
            t_cast = self.stats_cast(p, xsrc, deps)
        t_mm1 = None
        for c in range(NC):
            w = [t_cast]
            if c == 0 and self.ps_free is not None:
                w.append(self.ps_free)
            def m1(eng, c=c):
                return eng.matmul(self.ps_st[:, :T], self.ones_col_b[:, :],
                                  self.xb[:, c * T:c * T + T],
                                  start=(c == 0), stop=(c == NC - 1))
            t_mm1 = p.op("tensor", m1, waits=w,
                         inc="s_te" if c == NC - 1 else None)
        def sq_fn(eng):
            return eng.tensor_mul(out=self.xb[:, :NC * T],
                                  in0=self.xb[:, :NC * T],
                                  in1=self.xb[:, :NC * T])
        t_sq = p.op(self.sq_eng, sq_fn, waits=[t_mm1])
        t_sums = None
        for c in range(NC):
            def m2(eng, c=c):
                return eng.matmul(self.ps_st[:, T:], self.ones_col_b[:, :],
                                  self.xb[:, c * T:c * T + T],
                                  start=(c == 0), stop=(c == NC - 1))
            t_sums = p.op("tensor", m2, waits=[t_sq],
                          inc="s_te" if c == NC - 1 else None)
        self.xb_free = t_sums
        w = [t_sums, self.rows_free]
        def row1a(eng):
            eng.tensor_scalar_mul(self.r_mu[0:1, :], self.ps_st[0:1, :T], 1.0 / D)
            return eng.tensor_scalar_mul(self.r_var[0:1, :], self.ps_st[0:1, T:],
                                         1.0 / D)
        t_r1a = p.op("vector", row1a, waits=w, inc="s_dve")
        self.ps_free = t_r1a
        def row1b(eng):
            return eng.scalar_tensor_tensor(
                out=self.r_ab[0:1, self.T:], in0=self.r_mu[0:1, :], scalar=-1.0,
                in1=self.r_mu[0:1, :], op0=ALU.mult, op1=ALU.mult)
        p.op("vector", row1b, inc="s_dve")
        def row1c(eng):
            return eng.tensor_add(self.r_var[0:1, :], self.r_var[0:1, :],
                                  self.r_ab[0:1, self.T:])
        t_row1 = p.op("vector", row1c, inc="s_dve")
        def row2a(eng):
            return eng.activation(out=self.r_var[0:1, :], in_=self.r_var[0:1, :],
                                  func=AF.Ln, bias=self.c_eps[0:1, 0:1], scale=1.0)
        t_row2a = p.op("scalar", row2a, waits=[t_row1], inc="s_act")
        def row2b(eng):
            return eng.activation(out=self.r_ab[0:1, :self.T], in_=self.r_var[0:1, :],
                                  func=AF.Exp, bias=self.c_zero[0:1, 0:1], scale=-0.5)
        t_row2 = p.op("scalar", row2b, inc="s_act")
        def row3(eng):
            return eng.scalar_tensor_tensor(
                out=self.r_ab[0:1, self.T:], in0=self.r_mu[0:1, :], scalar=-1.0,
                in1=self.r_ab[0:1, :self.T], op0=ALU.mult, op1=ALU.mult)
        t_row3 = p.op("vector", row3, waits=[t_row2], inc="s_dve")
        par = self.inv % 2
        self.inv += 1
        if self.bcmode == "psum":
            self.pending.append((t_row3, par))
            return t_row3
        t_ro = p.dma("scalar", self.dram[par:par + 1, :], self.r_ab[0:1, :],
                     waits=[t_row3, self.dram_free[par]], inc=f"{self.tag}ro{par}")
        self.rows_free = t_ro
        self.pending.append((t_ro, par))
        return t_ro

    def bcast_in(self, p):
        T = self.T
        t_ro, par = self.pending.pop(0)
        if self.bcmode == "psum":
            def bcmm(eng):
                eng.matmul(self.bc[:, :T], self.ones_row_f[0:1, :],
                           self.r_ab[0:1, :T], start=True, stop=True)
                return eng.matmul(self.bc[:, T:], self.ones_row_f[0:1, :],
                                  self.r_ab[0:1, T:], start=True, stop=True)
            t_bi = p.op("tensor", bcmm, waits=[t_ro, self.bc_user], inc="s_te")
            self.dram_free[par] = t_bi
            self.rows_free = t_bi
            return t_bi
        src_bc = bass.AP(self.dram, par * 2 * T, [[0, 128], [1, 2 * T]])
        t_bi = p.dma("scalar", self.bc[:, :], src_bc,
                     waits=[t_ro, self.bc_user], inc=f"{self.tag}bi")
        self.dram_free[par] = t_bi
        return t_bi

    def set_bc_user(self, tok):
        self.bc_user = tok


def build(phases="ABC", dev=False):
    nc = bass.Bass("TRN2", target_bir_lowering=False)
    ctx = contextlib.ExitStack()
    p = Plan(nc, ctx)

    ei = lambda n, s, d: nc.dram_tensor(n, s, d, kind="ExternalInput")
    xT = ei("xT", [D, S], F32)
    W1 = ei("W1", [D, FF], F32)
    W2 = ei("W2", [FF, D], F32)
    b1d = ei("b1d", [128, FF // 128], F32)
    b2d = ei("b2d", [128, NC], F32)
    gammad = ei("gammad", [128, NC], F32)
    betad = ei("betad", [128, NC], F32)
    WqT = ei("WqT", [D, D], F32)
    WkT = ei("WkT", [D, D], F32)
    WvT = ei("WvT", [D, D], F32)
    bqd = ei("bqd", [128, NPAIR], F32)
    bkd = ei("bkd", [128, NPAIR], F32)
    bvF = ei("bvF", [1, D], F32)
    WoT = ei("WoT", [D, D], F32)
    bod = ei("bod", [128, NC], F32)

    last = phases[-1]
    def sc(name, shape, dt, out_when):
        kind = "ExternalOutput" if (dev and last == out_when) else "Internal"
        return nc.dram_tensor(name, shape, dt, kind=kind)
    W1bh = nc.dram_tensor("W1bh", [128, NC * FF], BF16, kind="Internal")
    W2bh = nc.dram_tensor("W2bh", [128, (FF // 128) * D], BF16, kind="Internal")
    x1T = sc("x1T", [D, S], F32, "A")
    ln1T = sc("ln1T", [D, S], BF16, "A")
    QsT = nc.dram_tensor("QsT", [D, S], BF16, kind="Internal")
    x2T = sc("x2T", [D, S], F32, "B")
    yT = nc.dram_tensor("yT", [D, S], F32,
                        kind="ExternalOutput" if "C" in phases else "Internal")

    sb = lambda *a: ctx.enter_context(nc.sbuf_tensor(*a))
    ones_col_b = sb("ones_col_b", [128, 1], BF16)
    ones_row_f = sb("ones_row_f", [1, 128], F32)
    c_eps = sb("c_eps", [1, 1], F32)
    c_zero = sb("c_zero", [1, 1], F32)
    c_log8 = sb("c_log8", [1, 1], F32)
    b1s = sb("b1s", [128, FF // 128], F32)
    b2s = sb("b2s", [128, NC], F32)
    gh = sb("gh", [128, NC], F32)
    bh = sb("bh", [128, NC], F32)
    bqs = sb("bqs", [128, NPAIR], F32)
    bks = sb("bks", [128, NPAIR], F32)
    bos = sb("bos", [128, NC], F32)
    qmax2 = sb("qmax2", [1, H], F32)
    kmax2 = sb("kmax2", [1, H], F32)
    mred = sb("mred", [1, 2], F32)
    negc = sb("negc", [1, H], F32)
    cln = sb("cln", [1, 2 * H], F32)
    cbias = sb("cbias", [128, H], F32)

    tprm = None
    for dst, src in ((b1s, b1d), (b2s, b2d), (gh, gammad), (bh, betad),
                     (bqs, bqd), (bks, bkd), (bos, bod)):
        tprm = p.dma("sync", dst[:, :], src[:, :], inc="s_prm")

    def init_fn(eng):
        eng.memset(ones_col_b[:, :], 1.0)
        eng.memset(ones_row_f[:, :], 1.0)
        eng.memset(c_eps[:, :], LN_EPS)
        eng.memset(c_zero[:, :], 0.0)
        eng.memset(c_log8[:, :], LOG_EIGHTH)
        eng.memset(qmax2[:, :], 0.0)
        eng.memset(kmax2[:, :], 0.0)
        eng.tensor_scalar_mul(gh[:, :], gh[:, :], 0.5)
        return eng.tensor_scalar_mul(bh[:, :], bh[:, :], 0.5)
    t_init = p.op("vector", init_fn, waits=[tprm], inc="s_dve")

    def ffn_phase(tag, in_x, in_tgt, out_x, out_ln, t_start, save_bf16=False,
                  bf16_src=False):
        es = contextlib.ExitStack()
        sbl = lambda *a: es.enter_context(nc.sbuf_tensor(*a))
        ptl = lambda *a: es.enter_context(nc.psum_tensor(*a))
        W1b = sbl(f"{tag}W1b", [128, NC * FF], BF16)
        W2b = sbl(f"{tag}W2b", [128, (FF // 128) * D], BF16)
        xstage = [sbl(f"{tag}xs{i}", [128, NC * TFF], F32) for i in range(3)]
        tgt = [sbl(f"{tag}tg{i}", [128, NC * TFF], BF16) for i in range(2)]
        x2st = sbl(f"{tag}x2", [128, NC * TFF], F32)
        wst = nc.alloc_sbuf_tensor_at(
            f"{tag}wst", [128, 2 * 1024], F32,
            offset=nc.lookup_mloc(x2st).addr)
        hsb = sbl(f"{tag}h", [128, 32 * TFF], BF16)
        f2scr = nc.alloc_sbuf_tensor_at(
            f"{tag}f2s", [128, NC * TFF], F32,
            offset=nc.lookup_mloc(hsb).addr + 16 * TFF * 2)
        xbB = sbl(f"{tag}xbB", [128, NC * TFF], BF16)
        sc = sbl(f"{tag}sc", [128, NC * TFF // 2], F32) if out_ln is not None else None
        prev_uw = [None]
        ps_h = [ptl(f"{tag}ph{i}", [128, TFF], F32) for i in range(2)]
        ps_eo = [ptl(f"{tag}pe{i}", [128, TFF], F32) for i in range(3)]
        lna = LNEmitter(p, nc, es, tag + "a", TFF, (ones_col_b, c_eps, c_zero),
                        ceng="gpsimd", sq_eng="vector")
        lnb = LNEmitter(p, nc, es, tag + "b", TFF, (ones_col_b, c_eps, c_zero),
                        xb=xbB, bcmode="psum", ones_row_f=ones_row_f)

        pre_x = {}
        for t in range(min(3, NTFF)):
            pre_x[t] = p.dma(
                "sync", xstage[t % 3].rearrange("p (c j) -> p c j", j=TFF),
                in_x.rearrange("(c p) t -> p c t", p=128)[:, :, t * TFF:(t + 1) * TFF],
                waits=[t_start], inc=f"{tag}_x{t % 3}")

        w1_tok = {}
        w2_tok = {}
        if bf16_src:
            for fb in range(4):
                toks = []
                for li, q in enumerate(("sync", "gpsimd")):
                    toks.append(p.dma(
                        q,
                        W1b.ap().rearrange("p (e f) -> p e f", f=FF)
                        [:, :, fb * 1024 + li * 512:fb * 1024 + (li + 1) * 512],
                        W1bh.ap().rearrange("p (e f) -> p e f", f=FF)
                        [:, :, fb * 1024 + li * 512:fb * 1024 + (li + 1) * 512],
                        waits=[t_start], inc=f"{tag}_wh{q[0]}{fb}"))
                for ec in range(NC):
                    w1_tok[(ec, fb)] = toks
            w2_all = [[p.dma("sync", W2b[:, :16 * D], W2bh[:, :16 * D],
                             waits=[t_start], inc=f"{tag}_w2hs"),
                       p.dma("gpsimd", W2b[:, 16 * D:], W2bh[:, 16 * D:],
                             waits=[t_start], inc=f"{tag}_w2hg")]]
        pend = []
        deferred = []
        ndma = [0]
        def flush_one():
            cfn, tdma, key, tokmap = deferred.pop(0)
            tcast = p.op("gpsimd", cfn, waits=[tdma], inc="s_gp")
            pend.append(tcast)
            if tokmap is not None:
                tokmap[key] = tcast
            return tcast
        def stream(dst, dsto, src, key, tokmap):
            i = ndma[0]
            ndma[0] += 1
            slot = i % 2
            w = [t_start] if i == 0 else []
            if i >= 2:
                w.append(pend[i - 2])
            tdma = p.dma("sync", wst[:, slot * 1024:slot * 1024 + 1024], src,
                         waits=w, inc=f"{tag}_w{slot}")
            def cfn(eng, dst=dst, dsto=dsto, slot=slot):
                return eng.tensor_copy(out=dst[:, dsto:dsto + 1024],
                                       in_=wst[:, slot * 1024:slot * 1024 + 1024])
            deferred.append((cfn, tdma, key, tokmap))
            if len(deferred) > 1:
                flush_one()
        if not bf16_src:
            for fb in range(FF // 1024):
                for ec in range(NC):
                    stream(W1b, ec * FF + fb * 1024,
                           W1[ec * 128:(ec + 1) * 128, fb * 1024:(fb + 1) * 1024],
                           (ec, fb), w1_tok)
            for fc in range(FF // 128):
                stream(W2b, fc * D,
                       W2[fc * 128:(fc + 1) * 128, :],
                       (fc, 0), w2_tok)
            w2_all = [flush_one()]
        if save_bf16:
            p.dma("gpsimd", W1bh[:, :], W1b[:, :], waits=[w2_all[0]],
                  inc=f"{tag}_wsv0")
            t_sv = p.dma("gpsimd", W2bh[:, :], W2b[:, :], waits=[w2_all[0]],
                         inc=f"{tag}_wsv1")

        tok_tile_done = [[t_start]] * NTFF
        tok_tgt_done = [None] * NTFF
        tok_eo_cons = {}
        tok_h_cons = [None] * NTFF
        t_lnout_free = None
        t_tgt_tok = [None] * NTFF
        prev_fin = [None]
        frontends = {}

        f1_bc = {}

        def frontend1(t):
            sl3 = t % 3
            t0 = t * TFF
            if t in pre_x:
                t_xin = pre_x[t]
            else:
                w = [t_start]
                if t >= 3:
                    w.extend(tok_tile_done[t - 3])
                t_xin = p.dma(
                    "sync", xstage[sl3].rearrange("p (c j) -> p c j", j=TFF),
                    in_x.rearrange("(c p) t -> p c t", p=128)[:, :, t0:t0 + TFF],
                    waits=w, inc=f"{tag}_x{sl3}")
            lna.stats(p, xstage[sl3], [t_xin])
            frontends[t] = t_xin

        bc_tok = {}

        def frontend2(t):
            sl3 = t % 3
            t_bc = bc_tok[t]
            def ap1(eng, sl3=sl3):
                return eng.tensor_mul(
                    out=f2scr[:, :], in0=xstage[sl3][:, :],
                    in1=rep_view(lna.bc, 0, TFF, NC))
            wap = [t_bc]
            if t >= 2:
                wap.append(tok_h_cons[t - 2])
            p.op("vector", ap1, waits=wap)
            def ap2(eng, t=t):
                return eng.tensor_add(
                    out=tgt[t % 2][:, :], in0=f2scr[:, :],
                    in1=rep_view(lna.bc, TFF, TFF, NC))
            wap2 = []
            if t >= 2 and tok_tgt_done[t - 2] is not None:
                wap2.append(tok_tgt_done[t - 2])
            t_tgt = p.op("vector", ap2, waits=wap2, inc="s_dve")
            lna.set_bc_user(t_tgt)
            t_tgt_tok[t] = t_tgt

        def backend_w1(t, flushes):
            t_tgt = t_tgt_tok[t]
            hrot = [ps_h[0], ps_h[1], ps_eo[0], ps_eo[1], ps_eo[2]]
            t_relu = [None] * 32
            for ft in range(32):
                hb = ft % 5
                wmm = [t_tgt, w1_tok[(NC - 1, ft // 8)]]
                if ft >= 5:
                    wmm.append(t_relu[ft - 5])
                if ft < 2 and t >= 1:
                    wmm.append(tok_h_cons[t - 1])
                if 2 <= ft < 5 and t >= 1:
                    wmm.append(tok_eo_cons.get((t - 1, [6, 7, 5][ft - 2])))
                def mmw1(eng, ft=ft, hb=hb, t=t):
                    r = None
                    for ec in range(NC):
                        r = eng.matmul(
                            hrot[hb][:, :],
                            W1b[:, ec * FF + ft * 128:ec * FF + ft * 128 + 128],
                            tgt[t % 2][:, ec * TFF:ec * TFF + TFF],
                            start=(ec == 0), stop=(ec == NC - 1))
                    return r
                tw1 = p.op("tensor", mmw1, waits=wmm, inc="s_te")
                def relu_fn(eng, ft=ft, hb=hb):
                    return eng.activation(
                        out=hsb[:, ft * TFF:ft * TFF + TFF],
                        in_=hrot[hb][:, :],
                        func=AF.Relu, bias=b1s[:, ft:ft + 1], scale=1.0)
                wrl = [tw1]
                if ft >= 16 and t + 1 < NTFF:
                    wrl.append(t_tgt_tok[t + 1])
                t_relu[ft] = p.op("scalar", relu_fn, waits=wrl, inc="s_act")
                if ft % 8 == 7 and flushes:
                    flushes.pop(0)()
            while flushes:
                flushes.pop(0)()
            tok_relu31[t] = t_relu[31]

        def tail_pieces(t):
            sl3 = t % 3
            t0 = t * TFF
            st = {}
            def p0():
                w = [tok_x2done[t]]
                if prev_lndma[0] is not None:
                    w.append(prev_lndma[0])
                st["c"] = lnb.stats_cast(p, x2st, w)
            def p1():
                lnb.stats(p, x2st, [], precast=st["c"])
            def p2():
                t_bc2 = lnb.bcast_in(p)
                def iap1(eng):
                    return eng.tensor_mul(
                        out=x2st[:, :], in0=x2st[:, :],
                        in1=rep_view(lnb.bc, 0, TFF, NC))
                p.op("vector", iap1, waits=[t_bc2, w2_all[0]])
                def iap2(eng):
                    return eng.tensor_add(
                        out=x2st[:, :], in0=x2st[:, :],
                        in1=rep_view(lnb.bc, TFF, TFF, NC))
                t_u = p.op("vector", iap2)
                t_fin_v = None
                t_fin_g = None
                for c in range(NC):
                    teng = "vector" if (t < 4 or c % 2 == 0) else "gpsimd"
                    def actg(eng, c=c):
                        return eng.tensor_scalar(
                            out=x2st[:, c * TFF:c * TFF + TFF],
                            in0=x2st[:, c * TFF:c * TFF + TFF],
                            scalar1=gh[:, c:c + 1], scalar2=bh[:, c:c + 1],
                            op0=ALU.mult, op1=ALU.add)
                    t_v = p.op(teng, actg, waits=[t_u])
                    def resid(eng, c=c, sl3=sl3):
                        return eng.tensor_add(
                            out=xstage[sl3][:, c * TFF:c * TFF + TFF],
                            in0=xstage[sl3][:, c * TFF:c * TFF + TFF],
                            in1=x2st[:, c * TFF:c * TFF + TFF])
                    tok = p.op(teng, resid, waits=[t_v])
                    if teng == "vector":
                        t_fin_v = tok
                    else:
                        t_fin_g = tok
                fins = [x for x in (t_fin_v, t_fin_g) if x is not None]
                pool_tok = t_fin_g if t_fin_g is not None else t_fin_v
                prev_uw[0] = pool_tok
                lnb.set_bc_user(pool_tok)
                prev_fin[0] = pool_tok
                t_x1_dma = p.dma(
                    "gpsimd",
                    out_x.rearrange("(c p) t -> p c t", p=128)[:, :, t0:t0 + TFF],
                    xstage[sl3].rearrange("p (c j) -> p c j", j=TFF),
                    waits=fins, inc=f"{tag}_ox{t % 2}")
                st["fins"] = fins
                st["x1"] = t_x1_dma
                if out_ln is None:
                    tok_tile_done[t] = [t_x1_dma]
            def p3():
                lnb.stats(p, xstage[sl3], st["fins"])
            def p4():
                t_bc3 = lnb.bcast_in(p)
                HW = NC * TFF // 2
                t_lno = None
                for h in range(2):
                    def lap1(eng, h=h, sl3=sl3):
                        return eng.tensor_mul(
                            out=sc[:, :], in0=xstage[sl3][:, h * HW:(h + 1) * HW],
                            in1=rep_view(lnb.bc, 0, TFF, NC // 2))
                    p.op("vector", lap1, waits=[t_bc3] if h == 0 else [])
                    def lap2(eng, h=h):
                        return eng.tensor_add(
                            out=xbB[:, h * HW:(h + 1) * HW], in0=sc[:, :],
                            in1=rep_view(lnb.bc, TFF, TFF, NC // 2))
                    t_lno = p.op("vector", lap2,
                                 inc="s_dve" if h == 1 else None)
                lnb.set_bc_user(t_lno)
                t_ln_dma = p.dma(
                    "gpsimd",
                    out_ln.rearrange("(c p) t -> p c t", p=128)[:, :, t0:t0 + TFF],
                    xbB.rearrange("p (c j) -> p c j", j=TFF),
                    waits=[t_lno], inc=f"{tag}_oln")
                prev_lndma[0] = t_ln_dma
                tok_tile_done[t] = [st["x1"], t_lno]
            if out_ln is not None:
                return [p0, p1, p2, p3, p4]
            return [p0, p1, p2]

        def backend_w2(t):
            t_w2 = [None] * NC
            t_evac = [None] * NC
            for dt in range(NC):
                wmm = [tok_relu31[t], w2_all[0], tok_eo_cons.get((t, dt - 3))]
                if dt < 3 and t >= 1:
                    wmm.append(tok_eo_cons.get((t - 1, dt + NC - 3)))
                def mmw2(eng, dt=dt):
                    r = None
                    for fc in range(32):
                        r = eng.matmul(
                            ps_eo[dt % 3][:, :],
                            W2b[:, fc * D + dt * 128:fc * D + dt * 128 + 128],
                            hsb[:, fc * TFF:fc * TFF + TFF],
                            start=(fc == 0), stop=(fc == 31))
                    return r
                t_w2[dt] = p.op("tensor", mmw2, waits=wmm, inc="s_te")
                def evac(eng, dt=dt, t=t):
                    return eng.scalar_tensor_tensor(
                        out=x2st[:, dt * TFF:dt * TFF + TFF],
                        in0=ps_eo[dt % 3][:, :],
                        scalar=b2s[:, dt:dt + 1],
                        in1=tgt[t % 2][:, dt * TFF:dt * TFF + TFF],
                        op0=ALU.add, op1=ALU.add)
                wev = [t_w2[dt], w2_all[0]]
                if t >= 1 and dt == 0:
                    wev.append(prev_fin[0])
                    wev.append(prev_uw[0])
                t_evac[dt] = p.op("vector", evac, waits=wev, inc="s_dve")
                tok_eo_cons[(t, dt)] = t_evac[dt]
            tok_x2done[t] = t_evac[NC - 1]
            tok_h_cons[t] = t_w2[NC - 1]
            tok_tgt_done[t] = t_evac[NC - 1]

        tok_relu31 = {}
        tok_x2done = {}
        prev_lndma = [None]
        for it in range(NTFF + 3):
            pieces = tail_pieces(it - 3) if it >= 3 else []
            if pieces:
                pieces.pop(0)()
            if 1 <= it <= NTFF:
                frontend2(it - 1)
            if 2 <= it <= NTFF + 1:
                backend_w1(it - 2, pieces)
            else:
                while pieces:
                    pieces.pop(0)()
            if 2 <= it <= NTFF + 1:
                backend_w2(it - 2)
            if it < NTFF:
                frontend1(it)
                bc_tok[it] = lna.bcast_in(p)
        es.close()

    def b_phase(t_start):
        es = contextlib.ExitStack()
        sbl = lambda *a: es.enter_context(nc.sbuf_tensor(*a))
        Kres = [sbl(f"Kr{i}", [128, S], BF16) for i in range(NPAIR)]
        Vres = sbl("Vr", [128, NKC * H * (DH + 1)], BF16)
        t_last = b1_sub(es, t_start, Kres, Vres)
        p.barrier()
        t_last = b2_sub(es, t_last, Kres, Vres)
        es.close()
        return t_last

    def b1_sub(es_outer, t_start, Kres, Vres):
        es = contextlib.ExitStack()
        sbl = lambda *a: es.enter_context(nc.sbuf_tensor(*a))
        ptl = lambda *a: es.enter_context(nc.psum_tensor(*a))
        bvs = sbl("bvs", [128, D], F32)
        tbv = p.dma("sync", bvs[:, :],
                    bass.AP(bvF, 0, [[0, 128], [1, D]]),
                    waits=[t_start], inc="s_prm")
        Wqb = sbl("B1Wqb", [128, NC * D], BF16)
        Wkb = sbl("B1Wkb", [128, NC * D], BF16)
        Wvb = sbl("B1Wvb", [128, NC * D], BF16)
        wst = sbl("B1wst", [128, 2 * 512], F32)
        wst2 = nc.alloc_sbuf_tensor_at(
            "B1wst2", [128, 2 * 512], F32, offset=nc.lookup_mloc(Vres).addr)
        lnst = [sbl(f"B1ln{i}", [128, NC * TB], BF16) for i in range(2)]
        qst = sbl("B1q", [128, 2 * TB], BF16)
        sqs = sbl("B1sq", [128, 2 * TB], BF16)
        ps_qk = [ptl(f"B1pqk{i}", [128, TB], F32) for i in range(4)]
        ps_v = [ptl(f"B1pv{i}", [128, TB], F32) for i in range(2)]
        ps_n = [ptl(f"B1pn{i}", [1, TB], F32) for i in range(2)]

        lanes = {
            "sync": {"wst": wst, "ceng": "scalar", "inc": "s_act",
                     "pend": [], "deferred": [], "n": 0},
            "gpsimd": {"wst": wst2, "ceng": "gpsimd", "inc": "s_gp",
                       "pend": [], "deferred": [], "n": 0},
        }
        wlast = {}
        def flush_one(q):
            L = lanes[q]
            cfn, tdma, wid = L["deferred"].pop(0)
            tcast = p.op(L["ceng"], cfn, waits=[tdma], inc=L["inc"])
            L["pend"].append(tcast)
            wlast.setdefault(wid, {})[q] = tcast
            return tcast
        def stream(q, dst, dsto, src, wid):
            L = lanes[q]
            i = L["n"]
            L["n"] += 1
            slot = i % 2
            w = [t_start] if i == 0 else []
            if i >= 2:
                w.append(L["pend"][i - 2])
            tdma = p.dma(q, L["wst"][:, slot * 512:slot * 512 + 512], src,
                         waits=w, inc=f"B1w{q[0]}{slot}")
            if L["ceng"] == "scalar":
                def cfn(eng, dst=dst, dsto=dsto, slot=slot, L=L):
                    return eng.activation(
                        out=dst[:, dsto:dsto + 512],
                        in_=L["wst"][:, slot * 512:slot * 512 + 512],
                        func=AF.Copy, bias=0.0, scale=1.0)
            else:
                def cfn(eng, dst=dst, dsto=dsto, slot=slot, L=L):
                    return eng.tensor_copy(out=dst[:, dsto:dsto + 512],
                                           in_=L["wst"][:, slot * 512:slot * 512 + 512])
            L["deferred"].append((cfn, tdma, wid))
            if len(L["deferred"]) > 1:
                flush_one(q)
        pre_ln = {}
        for t in range(2):
            pre_ln[t] = p.dma(
                "scalar", lnst[t].rearrange("p (c j) -> p c j", j=TB),
                ln1T.rearrange("(c p) t -> p c t", p=128)[:, :, t * TB:(t + 1) * TB],
                waits=[t_start], inc=f"B1ln{t}")
        ci = 0
        for wid, (Wb, WT) in enumerate(((Wqb, WqT), (Wkb, WkT), (Wvb, WvT))):
            for ec in range(NC):
                for db in range(D // 512):
                    q = "sync" if ci % 2 == 0 else "gpsimd"
                    ci += 1
                    stream(q, Wb, ec * D + db * 512,
                           WT[ec * 128:(ec + 1) * 128, db * 512:(db + 1) * 512],
                           wid)
        for q in lanes:
            while lanes[q]["deferred"]:
                flush_one(q)
        wq_all = list(wlast[0].values())
        wk_all = list(wlast[1].values())
        wv_all = list(wlast[2].values())

        def vones(eng):
            return eng.memset(
                Vres[:, :].rearrange("p (k h d) -> p k h d", h=H, d=DH + 1)
                [:, :, :, DH:DH + 1], 1.0)
        t_vones = p.op("gpsimd", vones, waits=list(wlast[2].values()), inc="s_gp")


        tok_ln_done = [None] * NTB
        t_qst_free = [None, None]
        rot_free = [None, None, None, None]
        vrot_free = [None, None]
        nfree = [None, None]
        sq_free = [None, None]
        t_ev = None

        for t in range(NTB):
            sl = t % 2
            t0 = t * TB
            if t < 2:
                t_ln = pre_ln[t]
            else:
                t_ln = p.dma(
                    "sync", lnst[sl].rearrange("p (c j) -> p c j", j=TB),
                    ln1T.rearrange("(c p) t -> p c t", p=128)[:, :, t0:t0 + TB],
                    waits=list(tok_ln_done[t - 2]), inc=f"B1ln{sl}")
            ln_users = []
            pending_norm = []
            for kind in range(2):
                Wb = Wqb if kind == 0 else Wkb
                bias = bqs if kind == 0 else bks
                m2 = qmax2 if kind == 0 else kmax2
                for pr in range(NPAIR):
                    if len(pending_norm) > 1:
                        pending_norm.pop(0)()
                    rot = (kind * NPAIR + pr) % 4
                    wmm = [t_ln, rot_free[rot]]
                    if t == 0 and pr == 0:
                        wmm.extend(wq_all if kind == 0 else wk_all)
                    def mmqk(eng, pr=pr, sl=sl, rot=rot, Wb=Wb):
                        r = None
                        for ec in range(NC):
                            r = eng.matmul(
                                ps_qk[rot][:, :],
                                Wb[:, ec * D + pr * 128:ec * D + pr * 128 + 128],
                                lnst[sl][:, ec * TB:ec * TB + TB],
                                start=(ec == 0), stop=(ec == NC - 1))
                        return r
                    t_mm = p.op("tensor", mmqk, waits=wmm, inc="s_te")
                    if kind == 0:
                        wev = [t_mm] + (t_qst_free[sl] or [])
                        dst = qst[:, sl * TB:sl * TB + TB]
                    else:
                        wev = [t_mm]
                        dst = Kres[pr][:, t0:t0 + TB]
                    def ev(eng, dst=dst, rot=rot, pr=pr, bias=bias):
                        return eng.tensor_scalar_add(dst, ps_qk[rot][:, :],
                                                     bias[:, pr:pr + 1])
                    t_ev = p.op("vector", ev, waits=wev, inc="s_dve")
                    rot_free[rot] = t_ev
                    ln_users.append(t_mm)
                    if kind == 0:
                        t_qdma = p.dma(
                            "gpsimd", QsT[pr * 128:(pr + 1) * 128, t0:t0 + TB],
                            qst[:, sl * TB:sl * TB + TB],
                            waits=[t_ev], inc=f"B1q{sl}")
                    sqb = (pr % 2) * TB
                    wsq = [t_ev] + (sq_free[pr % 2] or [])
                    def sqf(eng, dst=dst, sqb=sqb):
                        return eng.tensor_mul(out=sqs[:, sqb:sqb + TB],
                                              in0=dst, in1=dst)
                    t_sq = p.op("vector", sqf, waits=wsq, inc="s_dve")
                    if kind == 0:
                        t_qst_free[sl] = [t_qdma, t_sq]
                    def norm_thunk(pr=pr, sqb=sqb, t_sq=t_sq, m2=m2):
                        t_nm = [None, None]
                        for sub in range(2):
                            wn = [t_sq, nfree[sub]]
                            def nmm(eng, sqb=sqb, sub=sub):
                                return eng.matmul(
                                    ps_n[sub][:, :],
                                    ones_col_b[sub * 64:sub * 64 + 64, :],
                                    sqs[sub * 64:sub * 64 + 64, sqb:sqb + TB],
                                    start=True, stop=True,
                                    tile_position=(sub * 64, 0))
                            t_nm[sub] = p.op("tensor", nmm, waits=wn, inc="s_te")
                        sq_free[pr % 2] = t_nm
                        for sub in range(2):
                            hh = 2 * pr + sub
                            def red1(eng, sub=sub):
                                return eng.reduce_max(mred[0:1, sub:sub + 1],
                                                      ps_n[sub][:, :],
                                                      axis=mybir.AxisListType.X)
                            t_r1 = p.op("vector", red1, waits=[t_nm[sub]],
                                        inc="s_dve")
                            nfree[sub] = t_r1
                            def red2(eng, hh=hh, sub=sub, m2=m2):
                                return eng.tensor_max(m2[0:1, hh:hh + 1],
                                                      m2[0:1, hh:hh + 1],
                                                      mred[0:1, sub:sub + 1])
                            p.op("vector", red2, inc="s_dve")
                    pending_norm.append(norm_thunk)
            while pending_norm:
                pending_norm.pop(0)()
            for kci in range(TB // 128):
                kc = t * (TB // 128) + kci
                j0 = kci * 128
                for half in range(2):
                    vr = (kci * 2 + half) % 2
                    wmm = [t_ln, vrot_free[vr], t_vones if t == 0 else None,
                           tbv if t == 0 else None]
                    if t == 0:
                        wmm.extend(wv_all)
                    def mmv(eng, kc=kc, j0=j0, half=half, sl=sl, vr=vr):
                        r = None
                        for ec in range(NC):
                            r = eng.matmul(
                                ps_v[vr][:, :],
                                lnst[sl][:, ec * TB + j0:ec * TB + j0 + 128],
                                Wvb[:, ec * D + half * 512:ec * D + half * 512 + 512],
                                start=(ec == 0), stop=(ec == NC - 1))
                        return r
                    t_vmm = p.op("tensor", mmv, waits=wmm, inc="s_te")
                    ln_users.append(t_vmm)
                    def vev(eng, kc=kc, half=half, vr=vr):
                        return eng.tensor_add(
                            out=Vres[:, kc * 1040 + half * 8 * 65:]
                            .rearrange("p (h d) -> p h d", d=65)[:, 0:8, 0:64],
                            in0=ps_v[vr][:, :].rearrange("p (h d) -> p h d", d=64),
                            in1=bvs[:, half * 512:half * 512 + 512]
                            .rearrange("p (h d) -> p h d", d=64))
                    t_vev = p.op("vector", vev, waits=[t_vmm], inc="s_dve")
                    vrot_free[vr] = t_vev
            tok_ln_done[t] = [ln_users[-1]]
        es.close()
        return t_ev

    def b2_sub(es_outer, t_start, Kres, Vres):
        es = contextlib.ExitStack()
        sbl = lambda *a: es.enter_context(nc.sbuf_tensor(*a))
        ptl = lambda *a: es.enter_context(nc.psum_tensor(*a))
        Wob = sbl("B2Wob", [128, NC * D], BF16)
        wst = sbl("B2wst", [128, 2 * 512], F32)
        x1st = sbl("B2x1", [128, NC * TB], F32)
        x2st = x1st
        Oqb = sbl("B2O", [128, NPAIR * TB], BF16)
        Otmp = sbl("B2Ot", [64, TB], BF16)
        Rsb = sbl("B2R", [64, TB], F32)
        Rsb2 = sbl("B2R2", [64, TB], F32)
        rr2 = sbl("B2rr", [128, TB], F32)
        ones64 = sbl("B2one", [128, 64], F32)
        Qst = [sbl(f"B2q{i}", [128, TB], BF16) for i in range(2)]
        pT = [sbl(f"B2p{i}", [128, 2048], BF16) for i in range(6)]
        SR = ptl("B2sr", [128, 3072], F32)
        ps_o = [ptl("B2o1", [128, TB], F32), ptl("B2o2", [128, TB], F32)]

        rot_free = [None, None, None]
        rot_n = [0]
        def onesf(eng):
            eng.memset(ones64[:, :], 1.0)
            return eng.memset(rr2[:, :], 1.0)
        t_ones64 = p.op("vector", onesf, waits=[t_start], inc="s_dve")
        def take_slot():
            r = rot_n[0] % 3
            rot_n[0] += 1
            return r

        def st1(eng):
            return eng.activation(out=cln[0:1, 0:H], in_=qmax2[0:1, :],
                                  func=AF.Ln, bias=c_zero[0:1, 0:1], scale=1.0)
        t1 = p.op("scalar", st1, waits=[t_start], inc="s_act")
        def st2(eng):
            return eng.activation(out=cln[0:1, H:2 * H], in_=kmax2[0:1, :],
                                  func=AF.Ln, bias=c_zero[0:1, 0:1], scale=1.0)
        t2 = p.op("scalar", st2, inc="s_act")
        def st3(eng):
            return eng.tensor_add(negc[0:1, :], cln[0:1, 0:H], cln[0:1, H:2 * H])
        t3 = p.op("vector", st3, waits=[t2], inc="s_dve")
        def st4(eng):
            return eng.activation(out=negc[0:1, :], in_=negc[0:1, :],
                                  func=AF.Exp, bias=c_log8[0:1, 0:1], scale=0.5)
        t4 = p.op("scalar", st4, waits=[t3], inc="s_act")
        def st4b(eng):
            return eng.reduce_max(negc[0:1, 16:17] if False else cln[0:1, 0:1],
                                  negc[0:1, :], axis=mybir.AxisListType.X)
        t4b = p.op("vector", st4b, waits=[t4], inc="s_dve")
        def st5(eng):
            return eng.tensor_scalar_mul(cln[0:1, 0:1], cln[0:1, 0:1], -1.0)
        t5 = p.op("vector", st5, inc="s_dve")
        def st6(eng):
            return eng.matmul(SR[:, 0:1], ones_row_f[0:1, 0:128],
                              cln[0:1, 0:1], start=True, stop=True)
        t6 = p.op("tensor", st6, waits=[t5], inc="s_te")
        def st7(eng):
            return eng.copy(out=cbias[:, 0:1], in_=SR[:, 0:1])
        t_cb = p.op("scalar", st7, waits=[t6], inc="s_act")
        rot_free[0] = t_cb

        pend = [t_start, t_start]
        for jc in range(NC):
            for db in range(D // 512):
                slot = len(pend) % 2
                w = [pend[-2]]
                tdma = p.dma("scalar", wst[:, slot * 512:slot * 512 + 512],
                             WoT[jc * 128:(jc + 1) * 128, db * 512:(db + 1) * 512],
                             waits=w, inc=f"B2w{slot}")
                def cfn(eng, jc=jc, db=db, slot=slot):
                    return eng.tensor_copy(
                        out=Wob[:, jc * D + db * 512:jc * D + db * 512 + 512],
                        in_=wst[:, slot * 512:slot * 512 + 512])
                tcast = p.op("gpsimd", cfn, waits=[tdma], inc="s_gp")
                pend.append(tcast)
        wo_all = pend[-1]

        NG = 16
        tok_qb_done = [None] * NTB
        q_free = [None, None]
        o_free = [None, None]
        R_free = [None, None]
        rrow_free = [None]
        Rdram_free = [None, None]
        tail_par = 0
        Otmp_free = None
        t_x2 = None

        t_oshift = None
        t_n1 = None
        pending_avs = []
        pending_tail = [None]
        pending_tail2 = [None]
        pending_out = []
        last_x2 = [None]
        for qb in range(NTB):
            q0 = qb * TB
            while pending_avs:
                pending_avs.pop(0)()
            tq_pre = {}
            for pr in (0, 1):
                wq = [q_free[pr % 2]] if q_free[pr % 2] else []
                if qb == 0 and pr == 0:
                    wq.append(t_cb)
                tq_pre[pr] = p.dma("sync", Qst[pr % 2][:, :],
                                   QsT[pr * 128:(pr + 1) * 128, q0:q0 + TB],
                                   waits=wq, inc=f"B2qd{pr % 2}")
            t_x1 = None
            tail_tok = {}
            for pr in range(NPAIR):
                poff = (4 * (qb * NPAIR + pr)) % 6
                if pr == 1:
                    wx = [wo_all if qb == 0 else None]
                    if qb >= 1 and tok_qb_done[qb - 1] is not None:
                        wx.extend(tok_qb_done[qb - 1])
                    t_x1 = p.dma(
                        "sync", x1st.rearrange("p (c j) -> p c j", j=TB),
                        x1T.rearrange("(c p) t -> p c t", p=128)[:, :, q0:q0 + TB],
                        waits=wx, inc="B2x1d")
                sl = pr % 2
                if pr in tq_pre:
                    t_q = tq_pre[pr]
                else:
                    wq = [q_free[sl]] if q_free[sl] else []
                    t_q = p.dma("sync", Qst[sl][:, :],
                                QsT[pr * 128:(pr + 1) * 128, q0:q0 + TB],
                                waits=wq, inc=f"B2qd{sl}")
                t_exp = {}
                t_av = {}
                def emit_av(g, t_exp=t_exp, t_av=t_av, pr=pr, poff=poff):
                    psl = (g + poff) % 6
                    wav = [t_exp[g]]
                    if g == 0:
                        wav.extend(o_free)
                    def mmav(eng, g=g, pr=pr, psl=psl):
                        rr = None
                        for sub in range(2):
                            for i in range(2):
                                kc = g * 2 + i
                                rr = eng.matmul(
                                    ps_o[sub][0:65, :],
                                    Vres[:, kc * 1040 + (2 * pr + sub) * 65:
                                         kc * 1040 + (2 * pr + sub) * 65 + 65],
                                    pT[psl][:, sub * 1024 + i * TB:
                                            sub * 1024 + i * TB + TB],
                                    start=(kc == 0), stop=(kc == NKC - 1))
                        return rr
                    t_av[g] = p.op("tensor", mmav, waits=wav, inc="s_te")
                for g in range(NG):
                    if g == 1:
                        while pending_avs:
                            pending_avs.pop(0)()
                        if pending_tail[0] is not None:
                            pending_tail[0]()
                            pending_tail[0] = None
                    r0 = take_slot()
                    r1 = take_slot()
                    psl = (g + poff) % 6
                    wmm = [rot_free[r0], rot_free[r1]]
                    if g == 0:
                        wmm.append(t_q)
                    def mmsc0(eng, g=g, pr=pr, r0=r0):
                        rr = None
                        for i in range(2):
                            kc = g * 2 + i
                            rr = eng.matmul(
                                SR[:, r0 * 1024 + i * TB:r0 * 1024 + i * TB + TB],
                                Kres[pr][0:64, kc * 128:kc * 128 + 128],
                                Qst[pr % 2][0:64, :],
                                start=True, stop=True, tile_position=(0, 0))
                        return rr
                    t_sc0 = p.op("tensor", mmsc0, waits=wmm, inc="s_te")
                    def mmsc1(eng, g=g, pr=pr, r1=r1):
                        rr = None
                        for i in range(2):
                            kc = g * 2 + i
                            rr = eng.matmul(
                                SR[:, r1 * 1024 + i * TB:r1 * 1024 + i * TB + TB],
                                Kres[pr][64:128, kc * 128:kc * 128 + 128],
                                Qst[pr % 2][64:128, :],
                                start=True, stop=True, tile_position=(64, 0))
                        return rr
                    t_sc = p.op("tensor", mmsc1, inc="s_te")
                    wexp = [t_sc0, t_av.get(g - 6)]
                    def expf0(eng, psl=psl, r0=r0):
                        return eng.activation(
                            out=pT[psl][:, 0:1024],
                            in_=SR[:, r0 * 1024:r0 * 1024 + 1024],
                            func=AF.Exp, bias=cbias[:, 0:1], scale=0.125)
                    t_e0 = p.op("scalar", expf0, waits=wexp, inc="s_act")
                    rot_free[r0] = t_e0
                    def expf1(eng, psl=psl, r1=r1):
                        return eng.activation(
                            out=pT[psl][:, 1024:2048],
                            in_=SR[:, r1 * 1024:r1 * 1024 + 1024],
                            func=AF.Exp, bias=cbias[:, 0:1], scale=0.125)
                    t_e = p.op("scalar", expf1, waits=[t_sc], inc="s_act")
                    t_exp[g] = t_e
                    rot_free[r1] = t_e
                    if g == 3 and pending_tail2[0] is not None:
                        pending_tail2[0]()
                        pending_tail2[0] = None
                    if g >= 4 and pending_out:
                        pending_out.pop(0)()
                    if g >= 5:
                        emit_av(g - 5)
                emit_av(NG - 5)
                emit_av(NG - 4)
                emit_av(NG - 3)
                def av_tail(g, sl=sl, t_av=t_av, emit_av=emit_av):
                    def thunk():
                        emit_av(g)
                        if g == NG - 1:
                            q_free[sl] = t_av[NG - 1]
                    return thunk
                pending_avs.append(av_tail(NG - 2))
                pending_avs.append(av_tail(NG - 1))

                rc_hold = {}
                def emit_tail1(pr=pr, t_av=t_av, rc_hold=rc_hold):
                    t_avl = t_av[NG - 1]
                    def lc1(eng):
                        return eng.tensor_copy(out=rr2[0:1, :],
                                               in_=ps_o[0][64:65, :])
                    p.op("vector", lc1, waits=[t_avl, R_free[0], t_ones64])
                    def lc2(eng):
                        return eng.tensor_copy(out=rr2[64:65, :],
                                               in_=ps_o[1][64:65, :])
                    p.op("vector", lc2, waits=[t_avl, R_free[1]])
                    def rc(eng):
                        return eng.reciprocal(rr2[:, :], rr2[:, :])
                    t_rc = p.op("vector", rc, inc="s_dve")
                    rc_hold["rc"] = (t_rc, t_rc)
                def emit_tail2(pr=pr, tail_tok=tail_tok, rc_hold=rc_hold):
                    nonlocal R_free, Otmp_free, t_oshift, t_n1
                    t_rc1, t_rc2 = rc_hold["rc"]
                    r = take_slot()
                    def bc12(eng, r=r):
                        eng.matmul(SR[0:64, r * 1024:r * 1024 + TB],
                                   ones64[0:1, :],
                                   rr2[0:1, :], start=True, stop=True)
                        return eng.matmul(SR[0:64, r * 1024 + TB:r * 1024 + 2 * TB],
                                          ones64[64:65, :],
                                          rr2[64:65, :], start=True, stop=True,
                                          tile_position=(64, 0))
                    t_b12 = p.op("tensor", bc12, waits=[t_rc1, t_rc2, rot_free[r]],
                                 inc="s_te")
                    def bc1e(eng, r=r):
                        return eng.tensor_copy(out=Rsb[0:64, :],
                                               in_=SR[0:64, r * 1024:r * 1024 + TB])
                    t_b1e = p.op("vector", bc1e, waits=[t_b12, R_free[0]],
                                 inc="s_dve")
                    def bc2e(eng, r=r):
                        return eng.tensor_copy(out=Rsb2[0:64, :],
                                               in_=SR[0:64, r * 1024 + TB:r * 1024 + 2 * TB])
                    t_b2e = p.op("vector", bc2e, waits=[R_free[1]], inc="s_dve")
                    rot_free[r] = t_b2e
                    def nm1(eng, pr=pr):
                        return eng.tensor_mul(out=Oqb[0:64, pr * TB:pr * TB + TB],
                                              in0=ps_o[0][0:64, :], in1=Rsb[0:64, :])
                    t_n1 = p.op("vector", nm1, waits=[t_b1e], inc="s_dve")
                    o_free[0] = t_n1
                    wn2 = [t_b2e, Otmp_free]
                    def nm2(eng):
                        return eng.tensor_mul(out=Otmp[0:64, :],
                                              in0=ps_o[1][0:64, :], in1=Rsb2[0:64, :])
                    t_n2 = p.op("vector", nm2, waits=wn2, inc="s_dve")
                    o_free[1] = t_n2
                    R_free = [t_n1, t_n2]
                    t_sh = p.dma("gpsimd", Oqb[64:128, pr * TB:pr * TB + TB],
                                 Otmp[0:64, :], waits=[t_n2], inc="B2sh")
                    Otmp_free = t_sh
                    t_oshift = t_sh
                    tail_tok[pr] = (t_n1, t_sh)
                pending_tail[0] = emit_tail1
                pending_tail2[0] = emit_tail2
            ost = {"last": None}
            def mk_out(it, tail_tok=tail_tok, t_x1=t_x1):
                def thunk(it=it, tail_tok=tail_tok, t_x1=t_x1):
                    r = take_slot()
                    wmm = [tail_tok[NC - 2][0], tail_tok[NC - 2][1], rot_free[r]]
                    def mmo_a(eng, it=it, r=r):
                        rr = None
                        for jc in range(NC - 1):
                            rr = eng.matmul(
                                SR[:, r * 1024:r * 1024 + TB],
                                Wob[:, jc * D + it * 128:jc * D + it * 128 + 128],
                                Oqb[:, jc * TB:jc * TB + TB],
                                start=(jc == 0), stop=False)
                        return rr
                    p.op("tensor", mmo_a, waits=wmm)
                    def mmo_b(eng, it=it, r=r):
                        jc = NC - 1
                        return eng.matmul(
                            SR[:, r * 1024:r * 1024 + TB],
                            Wob[:, jc * D + it * 128:jc * D + it * 128 + 128],
                            Oqb[:, jc * TB:jc * TB + TB],
                            start=False, stop=True)
                    t_mo = p.op("tensor", mmo_b,
                                waits=[tail_tok[NC - 1][0], tail_tok[NC - 1][1]],
                                inc="s_te")
                    def evo(eng, it=it, r=r):
                        return eng.scalar_tensor_tensor(
                            out=x2st[:, it * TB:it * TB + TB],
                            in0=SR[:, r * 1024:r * 1024 + TB],
                            scalar=bos[:, it:it + 1],
                            in1=x1st[:, it * TB:it * TB + TB],
                            op0=ALU.add, op1=ALU.add)
                    t_ev = p.op("vector", evo, waits=[t_mo, t_x1], inc="s_dve")
                    rot_free[r] = t_ev
                    ost["last"] = t_ev
                return thunk
            def mk_x2(qb=qb, q0=q0, ost=ost):
                def thunk():
                    t_x2 = p.dma(
                        "gpsimd",
                        x2T.rearrange("(c p) t -> p c t", p=128)[:, :, q0:q0 + TB],
                        x2st.rearrange("p (c j) -> p c j", j=TB),
                        waits=[ost["last"]], inc="B2x2d")
                    tok_qb_done[qb] = [t_x2]
                    last_x2[0] = t_x2
                return thunk
            for it in range(NC):
                pending_out.append(mk_out(it))
            pending_out.append(mk_x2())
        while pending_avs:
            pending_avs.pop(0)()
        if pending_tail[0] is not None:
            pending_tail[0]()
            pending_tail[0] = None
        if pending_tail2[0] is not None:
            pending_tail2[0]()
            pending_tail2[0] = None
        while pending_out:
            pending_out.pop(0)()
        es.close()
        return last_x2[0]

    t0tok = t_init
    if "A" in phases:
        ffn_phase("A", xT, None, x1T, ln1T, t0tok, save_bf16=True)
        p.barrier()
    if "B" in phases:
        b_phase(t0tok)
        p.barrier()
    if "C" in phases:
        ffn_phase("C", x2T, None, yT, None, t0tok, bf16_src=True)
        p.barrier()
    p.barrier()

    p.emit()
    ctx.close()
    return nc



_NC_CACHE = {}


def _get_nc():
    if "nc" not in _NC_CACHE:
        _NC_CACHE["nc"] = build(phases="ABC", dev=False)
    return _NC_CACHE["nc"]


def kernel(L0_attn_hidden_state, L0_attn_Wq, L0_attn_bq, L0_attn_Wk, L0_attn_bk,
           L0_attn_Wv, L0_attn_bv, L0_attn_Wo, L0_attn_bo, L0_ffn_router_w,
           L0_ffn_router_b, L0_ffn_w1, L0_ffn_b1, L0_ffn_w2, L0_ffn_b2,
           L0_ffn_gamma, L0_ffn_beta):
    from concourse.bass_utils import run_bass_kernel_spmd
    x = np.asarray(L0_attn_hidden_state)
    B = x.shape[0]
    asc = np.ascontiguousarray
    shared = {
        "W1": asc(np.asarray(L0_ffn_w1)[0]),
        "W2": asc(np.asarray(L0_ffn_w2)[0]),
        "b1d": asc(np.asarray(L0_ffn_b1)[0].reshape(32, 128).T),
        "b2d": asc(np.asarray(L0_ffn_b2)[0].reshape(8, 128).T),
        "gammad": asc(np.asarray(L0_ffn_gamma).reshape(8, 128).T),
        "betad": asc(np.asarray(L0_ffn_beta).reshape(8, 128).T),
        "WqT": asc(np.asarray(L0_attn_Wq).transpose(2, 0, 1).reshape(1024, 1024)),
        "WkT": asc(np.asarray(L0_attn_Wk).transpose(2, 0, 1).reshape(1024, 1024)),
        "WvT": asc(np.asarray(L0_attn_Wv).transpose(2, 0, 1).reshape(1024, 1024)),
        "bqd": asc(np.asarray(L0_attn_bq).reshape(8, 128).T),
        "bkd": asc(np.asarray(L0_attn_bk).reshape(8, 128).T),
        "bvF": asc(np.asarray(L0_attn_bv).reshape(1, 1024)),
        "WoT": asc(np.asarray(L0_attn_Wo).T),
        "bod": asc(np.asarray(L0_attn_bo).reshape(8, 128).T),
    }
    in_maps = []
    for b in range(B):
        m = dict(shared)
        m["xT"] = asc(x[b].T)
        in_maps.append(m)
    nc = _get_nc()
    res = run_bass_kernel_spmd(nc, in_maps, core_ids=list(range(B)))
    out = np.empty((B, S, D), np.float32)
    for b in range(B):
        out[b] = res.results[b]["yT"].T
    return out


# version 30
# speedup vs baseline: 1.0484x; 1.0315x over previous
import contextlib
import numpy as np
import concourse.bass as bass
import concourse.mybir as mybir

F32 = mybir.dt.float32
BF16 = mybir.dt.bfloat16
AF = mybir.ActivationFunctionType
ALU = mybir.AluOpType

D = 1024
S = 4096
H = 16
DH = 64
FF = 4096
NC = D // 128
NPAIR = H // 2
TFF = 256
NTFF = S // TFF
TB = 512
NTB = S // TB
NKC = S // 128
LN_EPS = 1e-5
LOG_EIGHTH = float(np.log(0.125))


def rep_view(t, col0, T, n):
    W = t.shape[1]
    return bass.AP(t, col0, [[W, 128], [0, n], [1, T]])

ENGINES = ("sync", "tensor", "vector", "scalar", "gpsimd")


class Plan:

    def __init__(self, nc, ctx):
        self.nc = nc
        self.ctx = ctx
        self.entries = {e: [] for e in ENGINES}
        self.sems = {}
        self.counts = {}
        for s in ("s_te", "s_act", "s_dve", "s_gp"):
            self._mksem(s)

    def _mksem(self, s):
        if s not in self.sems:
            self.sems[s] = self.ctx.enter_context(self.nc.semaphore(s))
            self.counts[s] = 0

    _CHAIN = {"vector": "s_dve", "scalar": "s_act", "gpsimd": "s_gp"}

    def op(self, engine, fn, waits=(), inc=None, amt=1):
        flat = []
        for w in waits:
            if w is None:
                continue
            if isinstance(w, list):
                flat.extend(x for x in w if x is not None)
            else:
                flat.append(w)
        waits = flat
        if engine in self._CHAIN and fn is not None:
            s = self._CHAIN[engine]
            if inc is None:
                inc = s
            if self.counts.get(s, 0) > 0:
                waits.append((s, self.counts[s]))
        tok = None
        if inc is not None:
            self._mksem(inc)
            self.counts[inc] += amt
            tok = (inc, self.counts[inc])
        self.entries[engine].append((fn, waits, inc, amt))
        return tok

    def dma(self, engine, out, in_, waits=(), inc=None):
        assert inc is not None
        def fn(eng, out=out, in_=in_):
            return eng.dma_start(out=out, in_=in_)
        return self.op(engine, fn, waits=waits, inc=inc, amt=16)

    def wait(self, engine, waits):
        self.entries[engine].append((None, [w for w in waits if w is not None],
                                     None, 0))

    def barrier(self):
        toks = [(s, self.counts[s]) for s in self.sems if self.counts[s] > 0]
        for e in ENGINES:
            self.wait(e, toks)

    def emit(self):
        nc = self.nc
        with nc.Block() as block:
            def mk(name):
                chain = self._CHAIN.get(name)
                def run(eng):
                    hwm = {}
                    mirror = 0
                    for fn, waits, inc, amt in self.entries[name]:
                        for (sname, val) in waits:
                            if hwm.get(sname, -1) >= val:
                                continue
                            hwm[sname] = val
                            eng.wait_ge(self.sems[sname], val)
                        if fn is None:
                            continue
                        inst = fn(eng)
                        if inc is not None:
                            inst.then_inc(self.sems[inc], amt)
                            if inc == chain:
                                mirror += amt
                return run
            block.sync(mk("sync"))
            block.tensor(mk("tensor"))
            block.vector(mk("vector"))
            block.scalar(mk("scalar"))
            block.gpsimd(mk("gpsimd"))


class LNEmitter:

    def __init__(self, p, nc, ctx, tag, T, shared, xb=None, ceng="vector",
                 bcmode="dma", ones_row_f=None, sq_eng=None):
        self.p = p
        self.T = T
        self.tag = tag
        self.ceng = ceng
        self.sq_eng = sq_eng or ceng
        self.bcmode = bcmode
        self.ones_row_f = ones_row_f
        (self.ones_col_b, self.c_eps, self.c_zero) = shared
        sb = lambda *a: ctx.enter_context(nc.sbuf_tensor(*a))
        pt = lambda *a: ctx.enter_context(nc.psum_tensor(*a))
        self.xb = xb if xb is not None else sb(f"{tag}_lnxb", [128, NC * T], BF16)
        self.r_mu = sb(f"{tag}_rmu", [1, T], F32)
        self.r_var = sb(f"{tag}_rvar", [1, T], F32)
        self.r_ab = sb(f"{tag}_rab", [1, 2 * T], F32)
        self.ps_st = pt(f"{tag}_pst", [1, 2 * T], F32)
        if bcmode == "psum":
            self.bc = pt(f"{tag}_bc", [128, 2 * T], F32)
        else:
            self.bc = sb(f"{tag}_bc", [128, 2 * T], F32)
        self.dram = nc.dram_tensor(f"{tag}_rows", [2, 2 * T], F32, kind="Internal")
        self.xb_free = None
        self.ps_free = None
        self.rows_free = None
        self.dram_free = [None, None]
        self.bc_user = None
        self.inv = 0
        self.pending = []

    def stats_cast(self, p, xsrc, deps):
        w = list(deps) + [self.xb_free]
        def cast_fn(eng):
            return eng.tensor_copy(out=self.xb[:, :NC * self.T],
                                   in_=xsrc[:, :NC * self.T])
        return p.op(self.ceng, cast_fn, waits=w)

    def stats(self, p, xsrc, deps, precast=None):
        T = self.T
        if precast is not None:
            t_cast = precast
        else:
            t_cast = self.stats_cast(p, xsrc, deps)
        t_mm1 = None
        for c in range(NC):
            w = [t_cast]
            if c == 0 and self.ps_free is not None:
                w.append(self.ps_free)
            def m1(eng, c=c):
                return eng.matmul(self.ps_st[:, :T], self.ones_col_b[:, :],
                                  self.xb[:, c * T:c * T + T],
                                  start=(c == 0), stop=(c == NC - 1))
            t_mm1 = p.op("tensor", m1, waits=w,
                         inc="s_te" if c == NC - 1 else None)
        def sq_fn(eng):
            return eng.tensor_mul(out=self.xb[:, :NC * T],
                                  in0=self.xb[:, :NC * T],
                                  in1=self.xb[:, :NC * T])
        t_sq = p.op(self.sq_eng, sq_fn, waits=[t_mm1])
        t_sums = None
        for c in range(NC):
            def m2(eng, c=c):
                return eng.matmul(self.ps_st[:, T:], self.ones_col_b[:, :],
                                  self.xb[:, c * T:c * T + T],
                                  start=(c == 0), stop=(c == NC - 1))
            t_sums = p.op("tensor", m2, waits=[t_sq],
                          inc="s_te" if c == NC - 1 else None)
        self.xb_free = t_sums
        w = [t_sums, self.rows_free]
        def row1a(eng):
            eng.tensor_scalar_mul(self.r_mu[0:1, :], self.ps_st[0:1, :T], 1.0 / D)
            return eng.tensor_scalar_mul(self.r_var[0:1, :], self.ps_st[0:1, T:],
                                         1.0 / D)
        t_r1a = p.op("vector", row1a, waits=w, inc="s_dve")
        self.ps_free = t_r1a
        def row1b(eng):
            return eng.scalar_tensor_tensor(
                out=self.r_ab[0:1, self.T:], in0=self.r_mu[0:1, :], scalar=-1.0,
                in1=self.r_mu[0:1, :], op0=ALU.mult, op1=ALU.mult)
        p.op("vector", row1b, inc="s_dve")
        def row1c(eng):
            return eng.tensor_add(self.r_var[0:1, :], self.r_var[0:1, :],
                                  self.r_ab[0:1, self.T:])
        t_row1 = p.op("vector", row1c, inc="s_dve")
        def row2a(eng):
            return eng.activation(out=self.r_var[0:1, :], in_=self.r_var[0:1, :],
                                  func=AF.Ln, bias=self.c_eps[0:1, 0:1], scale=1.0)
        t_row2a = p.op("scalar", row2a, waits=[t_row1], inc="s_act")
        def row2b(eng):
            return eng.activation(out=self.r_ab[0:1, :self.T], in_=self.r_var[0:1, :],
                                  func=AF.Exp, bias=self.c_zero[0:1, 0:1], scale=-0.5)
        t_row2 = p.op("scalar", row2b, inc="s_act")
        def row3(eng):
            return eng.scalar_tensor_tensor(
                out=self.r_ab[0:1, self.T:], in0=self.r_mu[0:1, :], scalar=-1.0,
                in1=self.r_ab[0:1, :self.T], op0=ALU.mult, op1=ALU.mult)
        t_row3 = p.op("vector", row3, waits=[t_row2], inc="s_dve")
        par = self.inv % 2
        self.inv += 1
        if self.bcmode == "psum":
            self.pending.append((t_row3, par))
            return t_row3
        t_ro = p.dma("scalar", self.dram[par:par + 1, :], self.r_ab[0:1, :],
                     waits=[t_row3, self.dram_free[par]], inc=f"{self.tag}ro{par}")
        self.rows_free = t_ro
        self.pending.append((t_ro, par))
        return t_ro

    def bcast_in(self, p):
        T = self.T
        t_ro, par = self.pending.pop(0)
        if self.bcmode == "psum":
            def bcmm(eng):
                eng.matmul(self.bc[:, :T], self.ones_row_f[0:1, :],
                           self.r_ab[0:1, :T], start=True, stop=True)
                return eng.matmul(self.bc[:, T:], self.ones_row_f[0:1, :],
                                  self.r_ab[0:1, T:], start=True, stop=True)
            t_bi = p.op("tensor", bcmm, waits=[t_ro, self.bc_user], inc="s_te")
            self.dram_free[par] = t_bi
            self.rows_free = t_bi
            return t_bi
        src_bc = bass.AP(self.dram, par * 2 * T, [[0, 128], [1, 2 * T]])
        t_bi = p.dma("scalar", self.bc[:, :], src_bc,
                     waits=[t_ro, self.bc_user], inc=f"{self.tag}bi")
        self.dram_free[par] = t_bi
        return t_bi

    def set_bc_user(self, tok):
        self.bc_user = tok


def build(phases="ABC", dev=False):
    nc = bass.Bass("TRN2", target_bir_lowering=False)
    ctx = contextlib.ExitStack()
    p = Plan(nc, ctx)

    ei = lambda n, s, d: nc.dram_tensor(n, s, d, kind="ExternalInput")
    xT = ei("xT", [D, S], F32)
    W1 = ei("W1", [D, FF], F32)
    W2 = ei("W2", [FF, D], F32)
    b1d = ei("b1d", [128, FF // 128], F32)
    b2d = ei("b2d", [128, NC], F32)
    gammad = ei("gammad", [128, NC], F32)
    betad = ei("betad", [128, NC], F32)
    WqT = ei("WqT", [D, D], F32)
    WkT = ei("WkT", [D, D], F32)
    WvT = ei("WvT", [D, D], F32)
    bqd = ei("bqd", [128, NPAIR], F32)
    bkd = ei("bkd", [128, NPAIR], F32)
    bvF = ei("bvF", [1, D], F32)
    WoT = ei("WoT", [D, D], F32)
    bod = ei("bod", [128, NC], F32)

    last = phases[-1]
    def sc(name, shape, dt, out_when):
        kind = "ExternalOutput" if (dev and last == out_when) else "Internal"
        return nc.dram_tensor(name, shape, dt, kind=kind)
    W1bh = nc.dram_tensor("W1bh", [128, NC * FF], BF16, kind="Internal")
    W2bh = nc.dram_tensor("W2bh", [128, (FF // 128) * D], BF16, kind="Internal")
    x1T = sc("x1T", [D, S], F32, "A")
    ln1T = sc("ln1T", [D, S], BF16, "A")
    QsT = nc.dram_tensor("QsT", [D, S], BF16, kind="Internal")
    x2T = sc("x2T", [D, S], F32, "B")
    yT = nc.dram_tensor("yT", [D, S], F32,
                        kind="ExternalOutput" if "C" in phases else "Internal")

    sb = lambda *a: ctx.enter_context(nc.sbuf_tensor(*a))
    ones_col_b = sb("ones_col_b", [128, 1], BF16)
    ones_row_f = sb("ones_row_f", [1, 128], F32)
    c_eps = sb("c_eps", [1, 1], F32)
    c_zero = sb("c_zero", [1, 1], F32)
    c_log8 = sb("c_log8", [1, 1], F32)
    b1s = sb("b1s", [128, FF // 128], F32)
    b2s = sb("b2s", [128, NC], F32)
    gh = sb("gh", [128, NC], F32)
    bh = sb("bh", [128, NC], F32)
    bqs = sb("bqs", [128, NPAIR], F32)
    bks = sb("bks", [128, NPAIR], F32)
    bos = sb("bos", [128, NC], F32)
    qmax2 = sb("qmax2", [1, H], F32)
    kmax2 = sb("kmax2", [1, H], F32)
    mred = sb("mred", [1, 2], F32)
    negc = sb("negc", [1, H], F32)
    cln = sb("cln", [1, 2 * H], F32)
    cbias = sb("cbias", [128, H], F32)

    tprm = None
    for dst, src in ((b1s, b1d), (b2s, b2d), (gh, gammad), (bh, betad),
                     (bqs, bqd), (bks, bkd), (bos, bod)):
        tprm = p.dma("sync", dst[:, :], src[:, :], inc="s_prm")

    def init_fn(eng):
        eng.memset(ones_col_b[:, :], 1.0)
        eng.memset(ones_row_f[:, :], 1.0)
        eng.memset(c_eps[:, :], LN_EPS)
        eng.memset(c_zero[:, :], 0.0)
        eng.memset(c_log8[:, :], LOG_EIGHTH)
        eng.memset(qmax2[:, :], 0.0)
        eng.memset(kmax2[:, :], 0.0)
        eng.tensor_scalar_mul(gh[:, :], gh[:, :], 0.5)
        return eng.tensor_scalar_mul(bh[:, :], bh[:, :], 0.5)
    t_init = p.op("vector", init_fn, waits=[tprm], inc="s_dve")

    def ffn_phase(tag, in_x, in_tgt, out_x, out_ln, t_start, save_bf16=False,
                  bf16_src=False):
        es = contextlib.ExitStack()
        sbl = lambda *a: es.enter_context(nc.sbuf_tensor(*a))
        ptl = lambda *a: es.enter_context(nc.psum_tensor(*a))
        W1b = sbl(f"{tag}W1b", [128, NC * FF], BF16)
        W2b = sbl(f"{tag}W2b", [128, (FF // 128) * D], BF16)
        xstage = [sbl(f"{tag}xs{i}", [128, NC * TFF], F32) for i in range(3)]
        tgt = [sbl(f"{tag}tg{i}", [128, NC * TFF], BF16) for i in range(2)]
        x2st = sbl(f"{tag}x2", [128, NC * TFF], F32)
        wst = nc.alloc_sbuf_tensor_at(
            f"{tag}wst", [128, 2 * 1024], F32,
            offset=nc.lookup_mloc(x2st).addr)
        hsb = sbl(f"{tag}h", [128, 32 * TFF], BF16)
        f2scr = nc.alloc_sbuf_tensor_at(
            f"{tag}f2s", [128, NC * TFF], F32,
            offset=nc.lookup_mloc(hsb).addr + 16 * TFF * 2)
        xbB = sbl(f"{tag}xbB", [128, NC * TFF], BF16)
        sc = sbl(f"{tag}sc", [128, NC * TFF // 2], F32) if out_ln is not None else None
        prev_uw = [None]
        ps_h = [ptl(f"{tag}ph{i}", [128, TFF], F32) for i in range(2)]
        ps_eo = [ptl(f"{tag}pe{i}", [128, TFF], F32) for i in range(3)]
        lna = LNEmitter(p, nc, es, tag + "a", TFF, (ones_col_b, c_eps, c_zero),
                        ceng="gpsimd", sq_eng="vector")
        lnb = LNEmitter(p, nc, es, tag + "b", TFF, (ones_col_b, c_eps, c_zero),
                        xb=xbB, bcmode="psum", ones_row_f=ones_row_f)

        pre_x = {}
        for t in range(min(3, NTFF)):
            pre_x[t] = p.dma(
                "sync", xstage[t % 3].rearrange("p (c j) -> p c j", j=TFF),
                in_x.rearrange("(c p) t -> p c t", p=128)[:, :, t * TFF:(t + 1) * TFF],
                waits=[t_start], inc=f"{tag}_x{t % 3}")

        w1_tok = {}
        w2_tok = {}
        if bf16_src:
            for fb in range(4):
                toks = []
                for li, q in enumerate(("sync", "gpsimd")):
                    toks.append(p.dma(
                        q,
                        W1b.ap().rearrange("p (e f) -> p e f", f=FF)
                        [:, :, fb * 1024 + li * 512:fb * 1024 + (li + 1) * 512],
                        W1bh.ap().rearrange("p (e f) -> p e f", f=FF)
                        [:, :, fb * 1024 + li * 512:fb * 1024 + (li + 1) * 512],
                        waits=[t_start], inc=f"{tag}_wh{q[0]}{fb}"))
                for ec in range(NC):
                    w1_tok[(ec, fb)] = toks
            w2_all = [[p.dma("sync", W2b[:, :16 * D], W2bh[:, :16 * D],
                             waits=[t_start], inc=f"{tag}_w2hs"),
                       p.dma("gpsimd", W2b[:, 16 * D:], W2bh[:, 16 * D:],
                             waits=[t_start], inc=f"{tag}_w2hg")]]
        pend = []
        deferred = []
        ndma = [0]
        def flush_one():
            cfn, tdma, key, tokmap = deferred.pop(0)
            tcast = p.op("gpsimd", cfn, waits=[tdma], inc="s_gp")
            pend.append(tcast)
            if tokmap is not None:
                tokmap[key] = tcast
            return tcast
        def stream(dst, dsto, src, key, tokmap):
            i = ndma[0]
            ndma[0] += 1
            slot = i % 2
            w = [t_start] if i == 0 else []
            if i >= 2:
                w.append(pend[i - 2])
            tdma = p.dma("sync", wst[:, slot * 1024:slot * 1024 + 1024], src,
                         waits=w, inc=f"{tag}_w{slot}")
            def cfn(eng, dst=dst, dsto=dsto, slot=slot):
                return eng.tensor_copy(out=dst[:, dsto:dsto + 1024],
                                       in_=wst[:, slot * 1024:slot * 1024 + 1024])
            deferred.append((cfn, tdma, key, tokmap))
            if len(deferred) > 1:
                flush_one()
        if not bf16_src:
            for fb in range(FF // 1024):
                for ec in range(NC):
                    stream(W1b, ec * FF + fb * 1024,
                           W1[ec * 128:(ec + 1) * 128, fb * 1024:(fb + 1) * 1024],
                           (ec, fb), w1_tok)
            for fc in range(FF // 128):
                stream(W2b, fc * D,
                       W2[fc * 128:(fc + 1) * 128, :],
                       (fc, 0), w2_tok)
            w2_all = [flush_one()]
        if save_bf16:
            p.dma("gpsimd", W1bh[:, :], W1b[:, :], waits=[w2_all[0]],
                  inc=f"{tag}_wsv0")
            t_sv = p.dma("gpsimd", W2bh[:, :], W2b[:, :], waits=[w2_all[0]],
                         inc=f"{tag}_wsv1")

        tok_tile_done = [[t_start]] * NTFF
        tok_tgt_done = [None] * NTFF
        tok_eo_cons = {}
        tok_h_cons = [None] * NTFF
        t_lnout_free = None
        t_tgt_tok = [None] * NTFF
        prev_fin = [None]
        frontends = {}

        f1_bc = {}

        def frontend1(t):
            sl3 = t % 3
            t0 = t * TFF
            if t in pre_x:
                t_xin = pre_x[t]
            else:
                w = [t_start]
                if t >= 3:
                    w.extend(tok_tile_done[t - 3])
                t_xin = p.dma(
                    "sync", xstage[sl3].rearrange("p (c j) -> p c j", j=TFF),
                    in_x.rearrange("(c p) t -> p c t", p=128)[:, :, t0:t0 + TFF],
                    waits=w, inc=f"{tag}_x{sl3}")
            lna.stats(p, xstage[sl3], [t_xin])
            frontends[t] = t_xin

        bc_tok = {}

        def frontend2(t):
            sl3 = t % 3
            t_bc = bc_tok[t]
            def ap1(eng, sl3=sl3):
                return eng.tensor_mul(
                    out=f2scr[:, :], in0=xstage[sl3][:, :],
                    in1=rep_view(lna.bc, 0, TFF, NC))
            wap = [t_bc]
            if t >= 2:
                wap.append(tok_h_cons[t - 2])
            p.op("vector", ap1, waits=wap)
            def ap2(eng, t=t):
                return eng.tensor_add(
                    out=tgt[t % 2][:, :], in0=f2scr[:, :],
                    in1=rep_view(lna.bc, TFF, TFF, NC))
            wap2 = []
            if t >= 2 and tok_tgt_done[t - 2] is not None:
                wap2.append(tok_tgt_done[t - 2])
            t_tgt = p.op("vector", ap2, waits=wap2, inc="s_dve")
            lna.set_bc_user(t_tgt)
            t_tgt_tok[t] = t_tgt

        def backend_w1(t, flushes):
            t_tgt = t_tgt_tok[t]
            hrot = [ps_h[0], ps_h[1], ps_eo[0], ps_eo[1], ps_eo[2]]
            t_relu = [None] * 32
            for ft in range(32):
                hb = ft % 5
                wmm = [t_tgt, w1_tok[(NC - 1, ft // 8)]]
                if ft >= 5:
                    wmm.append(t_relu[ft - 5])
                if ft < 2 and t >= 1:
                    wmm.append(tok_h_cons[t - 1])
                if 2 <= ft < 5 and t >= 1:
                    wmm.append(tok_eo_cons.get((t - 1, [6, 7, 5][ft - 2])))
                def mmw1(eng, ft=ft, hb=hb, t=t):
                    r = None
                    for ec in range(NC):
                        r = eng.matmul(
                            hrot[hb][:, :],
                            W1b[:, ec * FF + ft * 128:ec * FF + ft * 128 + 128],
                            tgt[t % 2][:, ec * TFF:ec * TFF + TFF],
                            start=(ec == 0), stop=(ec == NC - 1))
                    return r
                tw1 = p.op("tensor", mmw1, waits=wmm, inc="s_te")
                def relu_fn(eng, ft=ft, hb=hb):
                    return eng.activation(
                        out=hsb[:, ft * TFF:ft * TFF + TFF],
                        in_=hrot[hb][:, :],
                        func=AF.Relu, bias=b1s[:, ft:ft + 1], scale=1.0)
                wrl = [tw1]
                if ft >= 16 and t + 1 < NTFF:
                    wrl.append(t_tgt_tok[t + 1])
                t_relu[ft] = p.op("scalar", relu_fn, waits=wrl, inc="s_act")
                if ft % 8 == 7 and flushes:
                    flushes.pop(0)()
            while flushes:
                flushes.pop(0)()
            tok_relu31[t] = t_relu[31]

        def tail_pieces(t):
            sl3 = t % 3
            t0 = t * TFF
            st = {}
            def p0():
                w = [tok_x2done[t]]
                if prev_lndma[0] is not None:
                    w.append(prev_lndma[0])
                st["c"] = lnb.stats_cast(p, x2st, w)
            def p1():
                lnb.stats(p, x2st, [], precast=st["c"])
            def p2():
                t_bc2 = lnb.bcast_in(p)
                def iap1(eng):
                    return eng.tensor_mul(
                        out=x2st[:, :], in0=x2st[:, :],
                        in1=rep_view(lnb.bc, 0, TFF, NC))
                p.op("vector", iap1, waits=[t_bc2, w2_all[0]])
                def iap2(eng):
                    return eng.tensor_add(
                        out=x2st[:, :], in0=x2st[:, :],
                        in1=rep_view(lnb.bc, TFF, TFF, NC))
                t_u = p.op("vector", iap2)
                t_fin_v = None
                t_fin_g = None
                for c in range(NC):
                    teng = "vector" if (t < 4 or c % 2 == 0) else "gpsimd"
                    def actg(eng, c=c):
                        return eng.tensor_scalar(
                            out=x2st[:, c * TFF:c * TFF + TFF],
                            in0=x2st[:, c * TFF:c * TFF + TFF],
                            scalar1=gh[:, c:c + 1], scalar2=bh[:, c:c + 1],
                            op0=ALU.mult, op1=ALU.add)
                    t_v = p.op(teng, actg, waits=[t_u])
                    def resid(eng, c=c, sl3=sl3):
                        return eng.tensor_add(
                            out=xstage[sl3][:, c * TFF:c * TFF + TFF],
                            in0=xstage[sl3][:, c * TFF:c * TFF + TFF],
                            in1=x2st[:, c * TFF:c * TFF + TFF])
                    tok = p.op(teng, resid, waits=[t_v])
                    if teng == "vector":
                        t_fin_v = tok
                    else:
                        t_fin_g = tok
                fins = [x for x in (t_fin_v, t_fin_g) if x is not None]
                pool_tok = t_fin_g if t_fin_g is not None else t_fin_v
                prev_uw[0] = pool_tok
                lnb.set_bc_user(pool_tok)
                prev_fin[0] = pool_tok
                t_x1_dma = p.dma(
                    "gpsimd",
                    out_x.rearrange("(c p) t -> p c t", p=128)[:, :, t0:t0 + TFF],
                    xstage[sl3].rearrange("p (c j) -> p c j", j=TFF),
                    waits=fins, inc=f"{tag}_ox{t % 2}")
                st["fins"] = fins
                st["x1"] = t_x1_dma
                if out_ln is None:
                    tok_tile_done[t] = [t_x1_dma]
            def p3():
                lnb.stats(p, xstage[sl3], st["fins"])
            def p4():
                t_bc3 = lnb.bcast_in(p)
                HW = NC * TFF // 2
                t_lno = None
                for h in range(2):
                    def lap1(eng, h=h, sl3=sl3):
                        return eng.tensor_mul(
                            out=sc[:, :], in0=xstage[sl3][:, h * HW:(h + 1) * HW],
                            in1=rep_view(lnb.bc, 0, TFF, NC // 2))
                    p.op("vector", lap1, waits=[t_bc3] if h == 0 else [])
                    def lap2(eng, h=h):
                        return eng.tensor_add(
                            out=xbB[:, h * HW:(h + 1) * HW], in0=sc[:, :],
                            in1=rep_view(lnb.bc, TFF, TFF, NC // 2))
                    t_lno = p.op("vector", lap2,
                                 inc="s_dve" if h == 1 else None)
                lnb.set_bc_user(t_lno)
                t_ln_dma = p.dma(
                    "gpsimd",
                    out_ln.rearrange("(c p) t -> p c t", p=128)[:, :, t0:t0 + TFF],
                    xbB.rearrange("p (c j) -> p c j", j=TFF),
                    waits=[t_lno], inc=f"{tag}_oln")
                prev_lndma[0] = t_ln_dma
                tok_tile_done[t] = [st["x1"], t_lno]
            if out_ln is not None:
                return [p0, p1, p2, p3, p4]
            return [p0, p1, p2]

        def backend_w2(t):
            t_w2 = [None] * NC
            t_evac = [None] * NC
            for dt in range(NC):
                wmm = [tok_relu31[t], w2_all[0], tok_eo_cons.get((t, dt - 3))]
                if dt < 3 and t >= 1:
                    wmm.append(tok_eo_cons.get((t - 1, dt + NC - 3)))
                def mmw2(eng, dt=dt):
                    r = None
                    for fc in range(32):
                        r = eng.matmul(
                            ps_eo[dt % 3][:, :],
                            W2b[:, fc * D + dt * 128:fc * D + dt * 128 + 128],
                            hsb[:, fc * TFF:fc * TFF + TFF],
                            start=(fc == 0), stop=(fc == 31))
                    return r
                t_w2[dt] = p.op("tensor", mmw2, waits=wmm, inc="s_te")
                def evac(eng, dt=dt, t=t):
                    return eng.scalar_tensor_tensor(
                        out=x2st[:, dt * TFF:dt * TFF + TFF],
                        in0=ps_eo[dt % 3][:, :],
                        scalar=b2s[:, dt:dt + 1],
                        in1=tgt[t % 2][:, dt * TFF:dt * TFF + TFF],
                        op0=ALU.add, op1=ALU.add)
                wev = [t_w2[dt], w2_all[0]]
                if t >= 1 and dt == 0:
                    wev.append(prev_fin[0])
                    wev.append(prev_uw[0])
                t_evac[dt] = p.op("vector", evac, waits=wev, inc="s_dve")
                tok_eo_cons[(t, dt)] = t_evac[dt]
            tok_x2done[t] = t_evac[NC - 1]
            tok_h_cons[t] = t_w2[NC - 1]
            tok_tgt_done[t] = t_evac[NC - 1]

        tok_relu31 = {}
        tok_x2done = {}
        prev_lndma = [None]
        for it in range(NTFF + 3):
            pieces = tail_pieces(it - 3) if it >= 3 else []
            if pieces:
                pieces.pop(0)()
            if 1 <= it <= NTFF:
                frontend2(it - 1)
            if 2 <= it <= NTFF + 1:
                backend_w1(it - 2, pieces)
            else:
                while pieces:
                    pieces.pop(0)()
            if 2 <= it <= NTFF + 1:
                backend_w2(it - 2)
            if it < NTFF:
                frontend1(it)
                bc_tok[it] = lna.bcast_in(p)
        es.close()

    def b_phase(t_start):
        es = contextlib.ExitStack()
        sbl = lambda *a: es.enter_context(nc.sbuf_tensor(*a))
        Kres = [sbl(f"Kr{i}", [128, S], BF16) for i in range(NPAIR)]
        Vres = sbl("Vr", [128, NKC * H * (DH + 1)], BF16)
        t_last = b1_sub(es, t_start, Kres, Vres)
        p.barrier()
        t_last = b2_sub(es, t_last, Kres, Vres)
        es.close()
        return t_last

    def b1_sub(es_outer, t_start, Kres, Vres):
        es = contextlib.ExitStack()
        sbl = lambda *a: es.enter_context(nc.sbuf_tensor(*a))
        ptl = lambda *a: es.enter_context(nc.psum_tensor(*a))
        bvs = sbl("bvs", [128, D], F32)
        tbv = p.dma("sync", bvs[:, :],
                    bass.AP(bvF, 0, [[0, 128], [1, D]]),
                    waits=[t_start], inc="s_prm")
        Wqb = sbl("B1Wqb", [128, NC * D], BF16)
        Wkb = sbl("B1Wkb", [128, NC * D], BF16)
        Wvb = sbl("B1Wvb", [128, NC * D], BF16)
        wst = sbl("B1wst", [128, 2 * 512], F32)
        wst2 = nc.alloc_sbuf_tensor_at(
            "B1wst2", [128, 2 * 512], F32, offset=nc.lookup_mloc(Vres).addr)
        lnst = [sbl(f"B1ln{i}", [128, NC * TB], BF16) for i in range(2)]
        qst = sbl("B1q", [128, 2 * TB], BF16)
        sqs = sbl("B1sq", [128, 2 * TB], BF16)
        ps_qk = [ptl(f"B1pqk{i}", [128, TB], F32) for i in range(4)]
        ps_v = [ptl(f"B1pv{i}", [128, TB], F32) for i in range(2)]
        ps_n = [ptl(f"B1pn{i}", [1, TB], F32) for i in range(2)]

        lanes = {
            "sync": {"wst": wst, "ceng": "scalar", "inc": "s_act",
                     "pend": [], "deferred": [], "n": 0},
            "gpsimd": {"wst": wst2, "ceng": "gpsimd", "inc": "s_gp",
                       "pend": [], "deferred": [], "n": 0},
        }
        wlast = {}
        def flush_one(q):
            L = lanes[q]
            cfn, tdma, wid = L["deferred"].pop(0)
            tcast = p.op(L["ceng"], cfn, waits=[tdma], inc=L["inc"])
            L["pend"].append(tcast)
            wlast.setdefault(wid, {})[q] = tcast
            return tcast
        def stream(q, dst, dsto, src, wid):
            L = lanes[q]
            i = L["n"]
            L["n"] += 1
            slot = i % 2
            w = [t_start] if i == 0 else []
            if i >= 2:
                w.append(L["pend"][i - 2])
            tdma = p.dma(q, L["wst"][:, slot * 512:slot * 512 + 512], src,
                         waits=w, inc=f"B1w{q[0]}{slot}")
            if L["ceng"] == "scalar":
                def cfn(eng, dst=dst, dsto=dsto, slot=slot, L=L):
                    return eng.activation(
                        out=dst[:, dsto:dsto + 512],
                        in_=L["wst"][:, slot * 512:slot * 512 + 512],
                        func=AF.Copy, bias=0.0, scale=1.0)
            else:
                def cfn(eng, dst=dst, dsto=dsto, slot=slot, L=L):
                    return eng.tensor_copy(out=dst[:, dsto:dsto + 512],
                                           in_=L["wst"][:, slot * 512:slot * 512 + 512])
            L["deferred"].append((cfn, tdma, wid))
            if len(L["deferred"]) > 1:
                flush_one(q)
        pre_ln = {}
        for t in range(2):
            pre_ln[t] = p.dma(
                "scalar", lnst[t].rearrange("p (c j) -> p c j", j=TB),
                ln1T.rearrange("(c p) t -> p c t", p=128)[:, :, t * TB:(t + 1) * TB],
                waits=[t_start], inc=f"B1ln{t}")
        ci = 0
        for wid, (Wb, WT) in enumerate(((Wqb, WqT), (Wkb, WkT), (Wvb, WvT))):
            for ec in range(NC):
                for db in range(D // 512):
                    q = "sync" if ci % 2 == 0 else "gpsimd"
                    ci += 1
                    stream(q, Wb, ec * D + db * 512,
                           WT[ec * 128:(ec + 1) * 128, db * 512:(db + 1) * 512],
                           wid)
        for q in lanes:
            while lanes[q]["deferred"]:
                flush_one(q)
        wq_all = list(wlast[0].values())
        wk_all = list(wlast[1].values())
        wv_all = list(wlast[2].values())

        def vones(eng):
            return eng.memset(
                Vres[:, :].rearrange("p (k h d) -> p k h d", h=H, d=DH + 1)
                [:, :, :, DH:DH + 1], 1.0)
        t_vones = p.op("gpsimd", vones, waits=list(wlast[2].values()), inc="s_gp")


        tok_ln_done = [None] * NTB
        t_qst_free = [None, None]
        rot_free = [None, None, None, None]
        vrot_free = [None, None]
        nfree = [None, None]
        sq_free = [None, None]
        t_ev = None

        for t in range(NTB):
            sl = t % 2
            t0 = t * TB
            if t < 2:
                t_ln = pre_ln[t]
            else:
                t_ln = p.dma(
                    "sync", lnst[sl].rearrange("p (c j) -> p c j", j=TB),
                    ln1T.rearrange("(c p) t -> p c t", p=128)[:, :, t0:t0 + TB],
                    waits=list(tok_ln_done[t - 2]), inc=f"B1ln{sl}")
            ln_users = []
            pending_norm = []
            for kind in range(2):
                Wb = Wqb if kind == 0 else Wkb
                bias = bqs if kind == 0 else bks
                m2 = qmax2 if kind == 0 else kmax2
                for pr in range(NPAIR):
                    if len(pending_norm) > 1:
                        pending_norm.pop(0)()
                    rot = (kind * NPAIR + pr) % 4
                    wmm = [t_ln, rot_free[rot]]
                    if t == 0 and pr == 0:
                        wmm.extend(wq_all if kind == 0 else wk_all)
                    def mmqk(eng, pr=pr, sl=sl, rot=rot, Wb=Wb):
                        r = None
                        for ec in range(NC):
                            r = eng.matmul(
                                ps_qk[rot][:, :],
                                Wb[:, ec * D + pr * 128:ec * D + pr * 128 + 128],
                                lnst[sl][:, ec * TB:ec * TB + TB],
                                start=(ec == 0), stop=(ec == NC - 1))
                        return r
                    t_mm = p.op("tensor", mmqk, waits=wmm, inc="s_te")
                    if kind == 0:
                        wev = [t_mm] + (t_qst_free[sl] or [])
                        dst = qst[:, sl * TB:sl * TB + TB]
                    else:
                        wev = [t_mm]
                        dst = Kres[pr][:, t0:t0 + TB]
                    def ev(eng, dst=dst, rot=rot, pr=pr, bias=bias):
                        return eng.tensor_scalar_add(dst, ps_qk[rot][:, :],
                                                     bias[:, pr:pr + 1])
                    t_ev = p.op("vector", ev, waits=wev, inc="s_dve")
                    rot_free[rot] = t_ev
                    ln_users.append(t_mm)
                    if kind == 0:
                        t_qdma = p.dma(
                            "gpsimd", QsT[pr * 128:(pr + 1) * 128, t0:t0 + TB],
                            qst[:, sl * TB:sl * TB + TB],
                            waits=[t_ev], inc=f"B1q{sl}")
                    sqb = (pr % 2) * TB
                    wsq = [t_ev] + (sq_free[pr % 2] or [])
                    def sqf(eng, dst=dst, sqb=sqb):
                        return eng.tensor_mul(out=sqs[:, sqb:sqb + TB],
                                              in0=dst, in1=dst)
                    t_sq = p.op("vector", sqf, waits=wsq, inc="s_dve")
                    if kind == 0:
                        t_qst_free[sl] = [t_qdma, t_sq]
                    def norm_thunk(pr=pr, sqb=sqb, t_sq=t_sq, m2=m2):
                        t_nm = [None, None]
                        for sub in range(2):
                            wn = [t_sq, nfree[sub]]
                            def nmm(eng, sqb=sqb, sub=sub):
                                return eng.matmul(
                                    ps_n[sub][:, :],
                                    ones_col_b[sub * 64:sub * 64 + 64, :],
                                    sqs[sub * 64:sub * 64 + 64, sqb:sqb + TB],
                                    start=True, stop=True,
                                    tile_position=(sub * 64, 0))
                            t_nm[sub] = p.op("tensor", nmm, waits=wn, inc="s_te")
                        sq_free[pr % 2] = t_nm
                        for sub in range(2):
                            hh = 2 * pr + sub
                            def red1(eng, sub=sub):
                                return eng.reduce_max(mred[0:1, sub:sub + 1],
                                                      ps_n[sub][:, :],
                                                      axis=mybir.AxisListType.X)
                            t_r1 = p.op("vector", red1, waits=[t_nm[sub]],
                                        inc="s_dve")
                            nfree[sub] = t_r1
                            def red2(eng, hh=hh, sub=sub, m2=m2):
                                return eng.tensor_max(m2[0:1, hh:hh + 1],
                                                      m2[0:1, hh:hh + 1],
                                                      mred[0:1, sub:sub + 1])
                            p.op("vector", red2, inc="s_dve")
                    pending_norm.append(norm_thunk)
            while pending_norm:
                pending_norm.pop(0)()
            for kci in range(TB // 128):
                kc = t * (TB // 128) + kci
                j0 = kci * 128
                for half in range(2):
                    vr = (kci * 2 + half) % 2
                    wmm = [t_ln, vrot_free[vr], t_vones if t == 0 else None,
                           tbv if t == 0 else None]
                    if t == 0:
                        wmm.extend(wv_all)
                    def mmv(eng, kc=kc, j0=j0, half=half, sl=sl, vr=vr):
                        r = None
                        for ec in range(NC):
                            r = eng.matmul(
                                ps_v[vr][:, :],
                                lnst[sl][:, ec * TB + j0:ec * TB + j0 + 128],
                                Wvb[:, ec * D + half * 512:ec * D + half * 512 + 512],
                                start=(ec == 0), stop=(ec == NC - 1))
                        return r
                    t_vmm = p.op("tensor", mmv, waits=wmm, inc="s_te")
                    ln_users.append(t_vmm)
                    def vev(eng, kc=kc, half=half, vr=vr):
                        return eng.tensor_add(
                            out=Vres[:, kc * 1040 + half * 8 * 65:]
                            .rearrange("p (h d) -> p h d", d=65)[:, 0:8, 0:64],
                            in0=ps_v[vr][:, :].rearrange("p (h d) -> p h d", d=64),
                            in1=bvs[:, half * 512:half * 512 + 512]
                            .rearrange("p (h d) -> p h d", d=64))
                    t_vev = p.op("vector", vev, waits=[t_vmm], inc="s_dve")
                    vrot_free[vr] = t_vev
            tok_ln_done[t] = [ln_users[-1]]
        es.close()
        return t_ev

    def b2_sub(es_outer, t_start, Kres, Vres):
        es = contextlib.ExitStack()
        sbl = lambda *a: es.enter_context(nc.sbuf_tensor(*a))
        ptl = lambda *a: es.enter_context(nc.psum_tensor(*a))
        Wob = sbl("B2Wob", [128, NC * D], BF16)
        wst = sbl("B2wst", [128, 2 * 512], F32)
        x1st = sbl("B2x1", [128, NC * TB], F32)
        x2st = x1st
        Oqb = sbl("B2O", [128, NPAIR * TB], BF16)
        Otmp = sbl("B2Ot", [64, TB], BF16)
        Rsb = sbl("B2R", [64, TB], F32)
        Rsb2 = sbl("B2R2", [64, TB], F32)
        rr2 = sbl("B2rr", [128, TB], F32)
        ones64 = sbl("B2one", [128, 64], F32)
        Qst = [sbl(f"B2q{i}", [128, TB], BF16) for i in range(2)]
        pT = [sbl(f"B2p{i}", [128, 2048], BF16) for i in range(6)]
        SR = ptl("B2sr", [128, 3072], F32)
        ps_o = [ptl("B2o1", [128, TB], F32), ptl("B2o2", [128, TB], F32)]

        rot_free = [None, None, None]
        rot_n = [0]
        def onesf(eng):
            eng.memset(ones64[:, :], 1.0)
            return eng.memset(rr2[:, :], 1.0)
        t_ones64 = p.op("vector", onesf, waits=[t_start], inc="s_dve")
        def take_slot():
            r = rot_n[0] % 3
            rot_n[0] += 1
            return r

        def st1(eng):
            return eng.activation(out=cln[0:1, 0:H], in_=qmax2[0:1, :],
                                  func=AF.Ln, bias=c_zero[0:1, 0:1], scale=1.0)
        t1 = p.op("scalar", st1, waits=[t_start], inc="s_act")
        def st2(eng):
            return eng.activation(out=cln[0:1, H:2 * H], in_=kmax2[0:1, :],
                                  func=AF.Ln, bias=c_zero[0:1, 0:1], scale=1.0)
        t2 = p.op("scalar", st2, inc="s_act")
        def st3(eng):
            return eng.tensor_add(negc[0:1, :], cln[0:1, 0:H], cln[0:1, H:2 * H])
        t3 = p.op("vector", st3, waits=[t2], inc="s_dve")
        def st4(eng):
            return eng.activation(out=negc[0:1, :], in_=negc[0:1, :],
                                  func=AF.Exp, bias=c_log8[0:1, 0:1], scale=0.5)
        t4 = p.op("scalar", st4, waits=[t3], inc="s_act")
        def st4b(eng):
            return eng.reduce_max(negc[0:1, 16:17] if False else cln[0:1, 0:1],
                                  negc[0:1, :], axis=mybir.AxisListType.X)
        t4b = p.op("vector", st4b, waits=[t4], inc="s_dve")
        def st5(eng):
            return eng.tensor_scalar_mul(cln[0:1, 0:1], cln[0:1, 0:1], -1.0)
        t5 = p.op("vector", st5, inc="s_dve")
        def st6(eng):
            return eng.matmul(SR[:, 0:1], ones_row_f[0:1, 0:128],
                              cln[0:1, 0:1], start=True, stop=True)
        t6 = p.op("tensor", st6, waits=[t5], inc="s_te")
        def st7(eng):
            return eng.copy(out=cbias[:, 0:1], in_=SR[:, 0:1])
        t_cb = p.op("scalar", st7, waits=[t6], inc="s_act")
        rot_free[0] = t_cb

        pend = [t_start, t_start]
        for jc in range(NC):
            for db in range(D // 512):
                slot = len(pend) % 2
                w = [pend[-2]]
                tdma = p.dma("scalar", wst[:, slot * 512:slot * 512 + 512],
                             WoT[jc * 128:(jc + 1) * 128, db * 512:(db + 1) * 512],
                             waits=w, inc=f"B2w{slot}")
                def cfn(eng, jc=jc, db=db, slot=slot):
                    return eng.tensor_copy(
                        out=Wob[:, jc * D + db * 512:jc * D + db * 512 + 512],
                        in_=wst[:, slot * 512:slot * 512 + 512])
                tcast = p.op("gpsimd", cfn, waits=[tdma], inc="s_gp")
                pend.append(tcast)
        wo_all = pend[-1]

        NG = 16
        tok_qb_done = [None] * NTB
        q_free = [None, None]
        o_free = [None, None]
        R_free = [None, None]
        rrow_free = [None]
        Rdram_free = [None, None]
        tail_par = 0
        Otmp_free = None
        t_x2 = None

        t_oshift = None
        t_n1 = None
        pending_avs = []
        pending_tail = [None]
        pending_tail2 = [None]
        pending_out = []
        last_x2 = [None]
        for qb in range(NTB):
            q0 = qb * TB
            while pending_avs:
                pending_avs.pop(0)()
            tq_pre = {}
            for pr in (0, 1):
                wq = [q_free[pr % 2]] if q_free[pr % 2] else []
                if qb == 0 and pr == 0:
                    wq.append(t_cb)
                tq_pre[pr] = p.dma("sync", Qst[pr % 2][:, :],
                                   QsT[pr * 128:(pr + 1) * 128, q0:q0 + TB],
                                   waits=wq, inc=f"B2qd{pr % 2}")
            t_x1 = None
            tail_tok = {}
            for pr in range(NPAIR):
                poff = (4 * (qb * NPAIR + pr)) % 6
                if pr == 1:
                    wx = [wo_all if qb == 0 else None]
                    if qb >= 1 and tok_qb_done[qb - 1] is not None:
                        wx.extend(tok_qb_done[qb - 1])
                    t_x1 = p.dma(
                        "sync", x1st.rearrange("p (c j) -> p c j", j=TB),
                        x1T.rearrange("(c p) t -> p c t", p=128)[:, :, q0:q0 + TB],
                        waits=wx, inc="B2x1d")
                sl = pr % 2
                if pr in tq_pre:
                    t_q = tq_pre[pr]
                else:
                    wq = [q_free[sl]] if q_free[sl] else []
                    t_q = p.dma("sync", Qst[sl][:, :],
                                QsT[pr * 128:(pr + 1) * 128, q0:q0 + TB],
                                waits=wq, inc=f"B2qd{sl}")
                t_exp = {}
                t_av = {}
                def emit_av(g, t_exp=t_exp, t_av=t_av, pr=pr, poff=poff):
                    psl = (g + poff) % 6
                    wav = [t_exp[g]]
                    if g == 0:
                        wav.extend(o_free)
                    def mmav(eng, g=g, pr=pr, psl=psl):
                        rr = None
                        for sub in range(2):
                            for i in range(2):
                                kc = g * 2 + i
                                rr = eng.matmul(
                                    ps_o[sub][0:65, :],
                                    Vres[:, kc * 1040 + (2 * pr + sub) * 65:
                                         kc * 1040 + (2 * pr + sub) * 65 + 65],
                                    pT[psl][:, sub * 1024 + i * TB:
                                            sub * 1024 + i * TB + TB],
                                    start=(kc == 0), stop=(kc == NKC - 1))
                        return rr
                    t_av[g] = p.op("tensor", mmav, waits=wav, inc="s_te")
                for g in range(NG):
                    if g == 1:
                        while pending_avs:
                            pending_avs.pop(0)()
                        if pending_tail[0] is not None:
                            pending_tail[0]()
                            pending_tail[0] = None
                    r0 = take_slot()
                    r1 = take_slot()
                    psl = (g + poff) % 6
                    wmm = [rot_free[r0], rot_free[r1]]
                    if g == 0:
                        wmm.append(t_q)
                    def mmsc0(eng, g=g, pr=pr, r0=r0):
                        rr = None
                        for i in range(2):
                            kc = g * 2 + i
                            rr = eng.matmul(
                                SR[:, r0 * 1024 + i * TB:r0 * 1024 + i * TB + TB],
                                Kres[pr][0:64, kc * 128:kc * 128 + 128],
                                Qst[pr % 2][0:64, :],
                                start=True, stop=True, tile_position=(0, 0))
                        return rr
                    t_sc0 = p.op("tensor", mmsc0, waits=wmm, inc="s_te")
                    def mmsc1(eng, g=g, pr=pr, r1=r1):
                        rr = None
                        for i in range(2):
                            kc = g * 2 + i
                            rr = eng.matmul(
                                SR[:, r1 * 1024 + i * TB:r1 * 1024 + i * TB + TB],
                                Kres[pr][64:128, kc * 128:kc * 128 + 128],
                                Qst[pr % 2][64:128, :],
                                start=True, stop=True, tile_position=(64, 0))
                        return rr
                    t_sc = p.op("tensor", mmsc1, inc="s_te")
                    wexp = [t_sc0, t_av.get(g - 6)]
                    def expf0(eng, psl=psl, r0=r0):
                        return eng.activation(
                            out=pT[psl][:, 0:1024],
                            in_=SR[:, r0 * 1024:r0 * 1024 + 1024],
                            func=AF.Exp, bias=cbias[:, 0:1], scale=0.125)
                    t_e0 = p.op("scalar", expf0, waits=wexp, inc="s_act")
                    rot_free[r0] = t_e0
                    def expf1(eng, psl=psl, r1=r1):
                        return eng.activation(
                            out=pT[psl][:, 1024:2048],
                            in_=SR[:, r1 * 1024:r1 * 1024 + 1024],
                            func=AF.Exp, bias=cbias[:, 0:1], scale=0.125)
                    t_e = p.op("scalar", expf1, waits=[t_sc], inc="s_act")
                    t_exp[g] = t_e
                    rot_free[r1] = t_e
                    if g == 3 and pending_tail2[0] is not None:
                        pending_tail2[0]()
                        pending_tail2[0] = None
                    if g >= 4 and pending_out:
                        pending_out.pop(0)()
                    if g >= 5:
                        emit_av(g - 5)
                emit_av(NG - 5)
                emit_av(NG - 4)
                emit_av(NG - 3)
                def av_tail(g, sl=sl, t_av=t_av, emit_av=emit_av):
                    def thunk():
                        emit_av(g)
                        if g == NG - 1:
                            q_free[sl] = t_av[NG - 1]
                    return thunk
                pending_avs.append(av_tail(NG - 2))
                pending_avs.append(av_tail(NG - 1))

                rc_hold = {}
                def emit_tail1(pr=pr, t_av=t_av, rc_hold=rc_hold):
                    t_avl = t_av[NG - 1]
                    def lc1(eng):
                        return eng.tensor_copy(out=rr2[0:1, :],
                                               in_=ps_o[0][64:65, :])
                    p.op("vector", lc1, waits=[t_avl, R_free[0], t_ones64])
                    def lc2(eng):
                        return eng.tensor_copy(out=rr2[64:65, :],
                                               in_=ps_o[1][64:65, :])
                    p.op("vector", lc2, waits=[t_avl, R_free[1]])
                    def rc(eng):
                        return eng.reciprocal(rr2[:, :], rr2[:, :])
                    t_rc = p.op("vector", rc, inc="s_dve")
                    rc_hold["rc"] = (t_rc, t_rc)
                def emit_tail2(pr=pr, tail_tok=tail_tok, rc_hold=rc_hold):
                    nonlocal R_free, Otmp_free, t_oshift, t_n1
                    t_rc1, t_rc2 = rc_hold["rc"]
                    r = take_slot()
                    def bc12(eng, r=r):
                        eng.matmul(SR[0:64, r * 1024:r * 1024 + TB],
                                   ones64[0:1, :],
                                   rr2[0:1, :], start=True, stop=True)
                        return eng.matmul(SR[0:64, r * 1024 + TB:r * 1024 + 2 * TB],
                                          ones64[64:65, :],
                                          rr2[64:65, :], start=True, stop=True,
                                          tile_position=(64, 0))
                    t_b12 = p.op("tensor", bc12, waits=[t_rc1, t_rc2, rot_free[r]],
                                 inc="s_te")
                    def bc1e(eng, r=r):
                        return eng.tensor_copy(out=Rsb[0:64, :],
                                               in_=SR[0:64, r * 1024:r * 1024 + TB])
                    t_b1e = p.op("vector", bc1e, waits=[t_b12, R_free[0]],
                                 inc="s_dve")
                    def bc2e(eng, r=r):
                        return eng.tensor_copy(out=Rsb2[0:64, :],
                                               in_=SR[0:64, r * 1024 + TB:r * 1024 + 2 * TB])
                    t_b2e = p.op("vector", bc2e, waits=[R_free[1]], inc="s_dve")
                    rot_free[r] = t_b2e
                    def nm1(eng, pr=pr):
                        return eng.tensor_mul(out=Oqb[0:64, pr * TB:pr * TB + TB],
                                              in0=ps_o[0][0:64, :], in1=Rsb[0:64, :])
                    t_n1 = p.op("vector", nm1, waits=[t_b1e], inc="s_dve")
                    o_free[0] = t_n1
                    wn2 = [t_b2e, Otmp_free]
                    def nm2(eng, pr=pr):
                        return eng.tensor_mul(out=Oqb[64:128, pr * TB:pr * TB + TB],
                                              in0=ps_o[1][0:64, :], in1=Rsb2[0:64, :])
                    t_n2 = p.op("vector", nm2, waits=wn2, inc="s_dve")
                    o_free[1] = t_n2
                    R_free = [t_n1, t_n2]
                    Otmp_free = t_n2
                    t_oshift = t_n2
                    tail_tok[pr] = (t_n1, t_n2)
                pending_tail[0] = emit_tail1
                pending_tail2[0] = emit_tail2
            ost = {"last": None}
            def mk_out(it, tail_tok=tail_tok, t_x1=t_x1):
                def thunk(it=it, tail_tok=tail_tok, t_x1=t_x1):
                    r = take_slot()
                    wmm = [tail_tok[NC - 2][0], tail_tok[NC - 2][1], rot_free[r]]
                    def mmo_a(eng, it=it, r=r):
                        rr = None
                        for jc in range(NC - 1):
                            rr = eng.matmul(
                                SR[:, r * 1024:r * 1024 + TB],
                                Wob[:, jc * D + it * 128:jc * D + it * 128 + 128],
                                Oqb[:, jc * TB:jc * TB + TB],
                                start=(jc == 0), stop=False)
                        return rr
                    p.op("tensor", mmo_a, waits=wmm)
                    def mmo_b(eng, it=it, r=r):
                        jc = NC - 1
                        return eng.matmul(
                            SR[:, r * 1024:r * 1024 + TB],
                            Wob[:, jc * D + it * 128:jc * D + it * 128 + 128],
                            Oqb[:, jc * TB:jc * TB + TB],
                            start=False, stop=True)
                    t_mo = p.op("tensor", mmo_b,
                                waits=[tail_tok[NC - 1][0], tail_tok[NC - 1][1]],
                                inc="s_te")
                    def evo(eng, it=it, r=r):
                        return eng.scalar_tensor_tensor(
                            out=x2st[:, it * TB:it * TB + TB],
                            in0=SR[:, r * 1024:r * 1024 + TB],
                            scalar=bos[:, it:it + 1],
                            in1=x1st[:, it * TB:it * TB + TB],
                            op0=ALU.add, op1=ALU.add)
                    t_ev = p.op("vector", evo, waits=[t_mo, t_x1], inc="s_dve")
                    rot_free[r] = t_ev
                    ost["last"] = t_ev
                return thunk
            def mk_x2(qb=qb, q0=q0, ost=ost):
                def thunk():
                    t_x2 = p.dma(
                        "gpsimd",
                        x2T.rearrange("(c p) t -> p c t", p=128)[:, :, q0:q0 + TB],
                        x2st.rearrange("p (c j) -> p c j", j=TB),
                        waits=[ost["last"]], inc="B2x2d")
                    tok_qb_done[qb] = [t_x2]
                    last_x2[0] = t_x2
                return thunk
            for it in range(NC):
                pending_out.append(mk_out(it))
            pending_out.append(mk_x2())
        while pending_avs:
            pending_avs.pop(0)()
        if pending_tail[0] is not None:
            pending_tail[0]()
            pending_tail[0] = None
        if pending_tail2[0] is not None:
            pending_tail2[0]()
            pending_tail2[0] = None
        while pending_out:
            pending_out.pop(0)()
        es.close()
        return last_x2[0]

    t0tok = t_init
    if "A" in phases:
        ffn_phase("A", xT, None, x1T, ln1T, t0tok, save_bf16=True)
        p.barrier()
    if "B" in phases:
        b_phase(t0tok)
        p.barrier()
    if "C" in phases:
        ffn_phase("C", x2T, None, yT, None, t0tok, bf16_src=True)
        p.barrier()
    p.barrier()

    p.emit()
    ctx.close()
    return nc



_NC_CACHE = {}


def _get_nc():
    if "nc" not in _NC_CACHE:
        _NC_CACHE["nc"] = build(phases="ABC", dev=False)
    return _NC_CACHE["nc"]


def kernel(L0_attn_hidden_state, L0_attn_Wq, L0_attn_bq, L0_attn_Wk, L0_attn_bk,
           L0_attn_Wv, L0_attn_bv, L0_attn_Wo, L0_attn_bo, L0_ffn_router_w,
           L0_ffn_router_b, L0_ffn_w1, L0_ffn_b1, L0_ffn_w2, L0_ffn_b2,
           L0_ffn_gamma, L0_ffn_beta):
    from concourse.bass_utils import run_bass_kernel_spmd
    x = np.asarray(L0_attn_hidden_state)
    B = x.shape[0]
    asc = np.ascontiguousarray
    shared = {
        "W1": asc(np.asarray(L0_ffn_w1)[0]),
        "W2": asc(np.asarray(L0_ffn_w2)[0]),
        "b1d": asc(np.asarray(L0_ffn_b1)[0].reshape(32, 128).T),
        "b2d": asc(np.asarray(L0_ffn_b2)[0].reshape(8, 128).T),
        "gammad": asc(np.asarray(L0_ffn_gamma).reshape(8, 128).T),
        "betad": asc(np.asarray(L0_ffn_beta).reshape(8, 128).T),
        "WqT": asc(np.asarray(L0_attn_Wq).transpose(2, 0, 1).reshape(1024, 1024)),
        "WkT": asc(np.asarray(L0_attn_Wk).transpose(2, 0, 1).reshape(1024, 1024)),
        "WvT": asc(np.asarray(L0_attn_Wv).transpose(2, 0, 1).reshape(1024, 1024)),
        "bqd": asc(np.asarray(L0_attn_bq).reshape(8, 128).T),
        "bkd": asc(np.asarray(L0_attn_bk).reshape(8, 128).T),
        "bvF": asc(np.asarray(L0_attn_bv).reshape(1, 1024)),
        "WoT": asc(np.asarray(L0_attn_Wo).T),
        "bod": asc(np.asarray(L0_attn_bo).reshape(8, 128).T),
    }
    in_maps = []
    for b in range(B):
        m = dict(shared)
        m["xT"] = asc(x[b].T)
        in_maps.append(m)
    nc = _get_nc()
    res = run_bass_kernel_spmd(nc, in_maps, core_ids=list(range(B)))
    out = np.empty((B, S, D), np.float32)
    for b in range(B):
        out[b] = res.results[b]["yT"].T
    return out
